# version 04
# baseline (speedup 1.0000x reference)
import contextlib
import numpy as np
import ml_dtypes
import concourse.bass as bass
import concourse.mybir as mybir
from concourse.bass_utils import run_bass_kernel_spmd

F32 = mybir.dt.float32
BF16 = mybir.dt.bfloat16
AF = mybir.ActivationFunctionType
ALU = mybir.AluOpType
AX = mybir.AxisListType

NCORES = 8
BPC = 2
T = 2048
NT = BPC * T
D = 1024
NFT = 8
NEG = np.float32(-1e38)
BF = ml_dtypes.bfloat16


class PB:

    def __init__(self, nc, ctx):
        self.nc = nc
        self.ctx = ctx
        self.sem = {}
        self.cnt = {}
        for ch in ("G", "Sy", "Sc", "T", "V", "S", "P"):
            self.sem[ch] = ctx.enter_context(nc.semaphore("ch_" + ch))
            self.cnt[ch] = 0

    def inc(self, ins, name, k=1):
        ins.then_inc(self.sem[name], k)
        self.cnt[name] += k
        return self.cnt[name]

    def val(self, name):
        return self.cnt[name]

    def wait(self, eng, name, v=None):
        eng.wait_ge(self.sem[name], self.cnt[name] if v is None else v)

    def wait_map(self, eng, w):
        for ch, v in w.items():
            eng.wait_ge(self.sem[ch], v)

    def barrier(self):
        nc = self.nc
        for eng in (nc.tensor, nc.vector, nc.scalar, nc.gpsimd, nc.sync):
            for ch, v in self.cnt.items():
                if v:
                    eng.wait_ge(self.sem[ch], v)

    def end(self):
        nc = self.nc
        for ch in ("G", "Sy", "Sc"):
            if self.cnt[ch]:
                nc.sync.wait_ge(self.sem[ch], self.cnt[ch])


def fmv(t):
    return t.rearrange("(c p) t -> p c t", p=128)


def _aux_consts():
    aux = {}
    aux["c_ones_col_f32"] = np.ones((128, 1), np.float32)
    aux["c_ones_col_bf"] = np.ones((128, 1), np.float32).astype(BF)
    aux["c_ones_row_f32"] = np.ones((1, 128), np.float32)
    aux["c_ones_row_bf"] = np.ones((1, 128), np.float32).astype(BF)
    aux["c_row8_f32"] = np.full((1, 128), 8.0, np.float32)
    aux["c_idn_bf"] = np.eye(128, dtype=np.float32).astype(BF)
    aux["c_idn_f32"] = np.eye(128, dtype=np.float32)

    dh, win, n = 64, 256, T
    half = np.arange(0, dh, 2, dtype=np.float64)
    inv_freq = 1.0 / 10000.0 ** (half / dh)
    tt = np.arange(n, dtype=np.float64)
    freqs = tt[:, None] * inv_freq[None, :]
    freqs = np.concatenate([freqs, freqs], -1)
    base_scale = (half + 0.4 * dh) / (1.4 * dh)
    power = (tt - n // 2) / (win // 2)
    scale = base_scale[None, :] ** power[:, None]
    scale = np.concatenate([scale, scale], -1)
    sign = np.concatenate([-np.ones(32), np.ones(32)])[None, :]

    def tile_tok(a):
        return np.ascontiguousarray(
            a.reshape(16, 128, dh).transpose(1, 0, 2).astype(np.float32))
    aux["c_rot_csq"] = tile_tok(np.cos(freqs) * scale)
    aux["c_rot_ssq"] = tile_tok(np.sin(freqs) * scale * sign)
    aux["c_rot_cik"] = tile_tok(np.cos(freqs) / scale)
    aux["c_rot_sik"] = tile_tok(np.sin(freqs) / scale * sign)

    i = np.arange(128)[:, None]
    aux["c_am0"] = np.where(np.arange(128)[None, :] <= i, 0.0, NEG
                            ).astype(np.float32).astype(BF)
    aux["c_am1"] = np.where(np.arange(256)[None, :] <= i + 128, 0.0, NEG
                            ).astype(np.float32).astype(BF)
    j2 = np.arange(384)[None, :]
    aux["c_am2"] = np.where((j2 >= i) & (j2 <= i + 256), 0.0, NEG
                            ).astype(np.float32).astype(BF)

    iu = np.arange(128)
    aux["c_triu_inc_f32"] = (iu[:, None] <= iu[None, :]).astype(np.float32)
    aux["c_triu_strict_f32"] = (iu[:, None] < iu[None, :]).astype(np.float32)
    e127 = np.zeros((128, 1), np.float32); e127[127, 0] = 1.0
    aux["c_e127_f32"] = e127
    return aux


AUX_SPEC = {
    "c_ones_col_f32": ([128, 1], F32), "c_ones_col_bf": ([128, 1], BF16),
    "c_ones_row_f32": ([1, 128], F32), "c_ones_row_bf": ([1, 128], BF16),
    "c_row8_f32": ([1, 128], F32),
    "c_idn_bf": ([128, 128], BF16), "c_idn_f32": ([128, 128], F32),
    "c_rot_csq": ([128, 16, 64], F32), "c_rot_ssq": ([128, 16, 64], F32),
    "c_rot_cik": ([128, 16, 64], F32), "c_rot_sik": ([128, 16, 64], F32),
    "c_am0": ([128, 128], BF16), "c_am1": ([128, 256], BF16),
    "c_am2": ([128, 384], BF16),
    "c_triu_inc_f32": ([128, 128], F32), "c_triu_strict_f32": ([128, 128], F32),
    "c_e127_f32": ([128, 1], F32),
}


def _host_prep(inp):
    g = lambda k: np.ascontiguousarray(np.asarray(inp[k], dtype=np.float32))
    shared = _aux_consts()

    su, sd = g("L0_ffn_shared_up"), g("L0_ffn_shared_down")
    ru, rd = g("L0_ffn_routed_up"), g("L0_ffn_routed_down")
    shared["l0f_up"] = np.ascontiguousarray(
        np.concatenate([su, ru], 0).transpose(0, 2, 1))
    shared["l0f_down"] = np.ascontiguousarray(
        np.concatenate([sd, rd], 0).transpose(0, 2, 1))
    shared["l0f_rw"] = np.ascontiguousarray(g("L0_ffn_router_w").T)

    shared["l0a_wqT"] = np.ascontiguousarray(g("L0_attn_Wq").T)
    shared["l0a_wkvdT"] = np.ascontiguousarray(g("L0_attn_Wkv_down").T)
    shared["l0a_wkuT"] = np.ascontiguousarray(g("L0_attn_Wk_up").T)
    shared["l0a_wvuT"] = np.ascontiguousarray(g("L0_attn_Wv_up").T)
    shared["l0a_woT"] = np.ascontiguousarray(g("L0_attn_Wo").T)

    shared["l1a_w4T"] = np.ascontiguousarray(np.concatenate(
        [g("L1_attn_Wq").T, g("L1_attn_Wk").T, g("L1_attn_Wv").T,
         g("L1_attn_Wg").T], axis=1))
    shared["l1a_b4"] = np.ascontiguousarray(np.concatenate(
        [g("L1_attn_bq"), g("L1_attn_bk"), g("L1_attn_bv"),
         g("L1_attn_bg")])[None, :])
    shared["l1a_wabT"] = np.ascontiguousarray(np.concatenate(
        [g("L1_attn_Wa").T, g("L1_attn_Wb").T], axis=1))
    shared["l1a_bab"] = np.ascontiguousarray(np.concatenate(
        [g("L1_attn_ba"), g("L1_attn_bb")])[None, :])
    shared["l1a_woT"] = np.ascontiguousarray(g("L1_attn_Wo").T)
    shared["l1a_bo_fm"] = np.ascontiguousarray(g("L1_attn_bo").reshape(8, 128).T)

    shared["l1f_wqkv"] = g("L1_ffn_Wqkv")
    shared["l1f_wo"] = g("L1_ffn_Wo")
    shared["l1f_w1"] = g("L1_ffn_W1")
    shared["l1f_w2"] = g("L1_ffn_W2")
    shared["l1f_qs"] = np.ascontiguousarray(g("L1_ffn_q_scale")[:, None, :])
    shared["l1f_ks"] = np.ascontiguousarray(g("L1_ffn_k_scale")[:, None, :])
    shared["l1f_lng_fm"] = np.ascontiguousarray(
        g("L1_ffn_ln_g").reshape(6, 8, 128).transpose(0, 2, 1))
    shared["l1f_lnb_fm"] = np.ascontiguousarray(
        g("L1_ffn_ln_b").reshape(6, 8, 128).transpose(0, 2, 1))
    b1 = g("L1_ffn_b1")
    b1a = np.zeros((6, 2816), np.float32); b1a[:, :2730] = b1[:, :2730]
    b1g = np.zeros((6, 2816), np.float32); b1g[:, :2730] = b1[:, 2730:]
    shared["l1f_b1a"] = np.ascontiguousarray(b1a.reshape(6, 22, 128).transpose(0, 2, 1))
    shared["l1f_b1g"] = np.ascontiguousarray(b1g.reshape(6, 22, 128).transpose(0, 2, 1))
    shared["l1f_b2_fm"] = np.ascontiguousarray(
        g("L1_ffn_b2").reshape(6, 8, 128).transpose(0, 2, 1))

    x = g("L0_attn_x")
    xs = [np.ascontiguousarray(x[2 * c:2 * c + 2].reshape(NT, D).T)
          for c in range(NCORES)]
    return shared, xs


WEIGHT_SPECS = {
    "l0f_up": ([18, 1024, 512], F32), "l0f_down": ([18, 512, 1024], F32),
    "l0f_rw": ([1024, 16], F32),
    "l0a_wqT": ([1024, 2048], F32), "l0a_wkvdT": ([1024, 64], F32),
    "l0a_wkuT": ([64, 2048], F32), "l0a_wvuT": ([64, 2048], F32),
    "l0a_woT": ([2048, 1024], F32),
    "l1a_w4T": ([1024, 512], F32), "l1a_b4": ([1, 512], F32),
    "l1a_wabT": ([1024, 2], F32), "l1a_bab": ([1, 2], F32),
    "l1a_woT": ([128, 1024], F32), "l1a_bo_fm": ([128, 8], F32),
    "l1f_wqkv": ([6, 1024, 3072], F32), "l1f_wo": ([6, 1024, 1024], F32),
    "l1f_w1": ([6, 1024, 5460], F32), "l1f_w2": ([6, 2730, 1024], F32),
    "l1f_qs": ([6, 1, 64], F32), "l1f_ks": ([6, 1, 64], F32),
    "l1f_lng_fm": ([6, 128, 8], F32), "l1f_lnb_fm": ([6, 128, 8], F32),
    "l1f_b1a": ([6, 128, 22], F32), "l1f_b1g": ([6, 128, 22], F32),
    "l1f_b2_fm": ([6, 128, 8], F32),
}


class Consts:

    def __init__(self, p, E, names):
        nc = p.nc
        name2aux = {
            "ones_col_f32": "c_ones_col_f32", "ones_col_bf": "c_ones_col_bf",
            "ones_row_f32": "c_ones_row_f32", "ones_row_bf": "c_ones_row_bf",
            "row8_f32": "c_row8_f32", "idn_bf": "c_idn_bf", "idn_f32": "c_idn_f32",
            "rot_csq": "c_rot_csq", "rot_ssq": "c_rot_ssq",
            "rot_cik": "c_rot_cik", "rot_sik": "c_rot_sik",
            "am0": "c_am0", "am1": "c_am1", "am2": "c_am2",
            "triu_inc": "c_triu_inc_f32", "triu_strict": "c_triu_strict_f32",
            "e127": "c_e127_f32",
        }
        self.t = {}
        for n in names:
            an = name2aux[n]
            shp, dt = AUX_SPEC[an]
            dr = p.nc.dram_tensor(an, shp, dt, kind="ExternalInput").ap()
            sb = E(nc.sbuf_tensor("cst_" + n, shp, dt))
            p.inc(nc.gpsimd.dma_start(sb[:], dr), "G", 16)
            self.t[n] = sb
        self.ready = p.val("G")

    def barrier(self, p):
        nc = p.nc
        for eng in (nc.tensor, nc.vector, nc.scalar, nc.gpsimd, nc.sync):
            eng.wait_ge(p.sem["G"], self.ready)

    def __getattr__(self, n):
        if n in ("t", "ready"):
            raise AttributeError(n)
        return self.t[n]


class LNTile:
    def __init__(self, p, E, cst, tag, psum=None, maxw=512):
        nc = p.nc
        self.p, self.cst = p, cst
        self.maxw = maxw
        self.xbf = E(nc.sbuf_tensor(f"{tag}_xbf", [128, 8, maxw], BF16))
        self.sq = E(nc.sbuf_tensor(f"{tag}_sq", [128, 8, maxw], BF16))
        self.rows = E(nc.sbuf_tensor(f"{tag}_rows", [1, maxw * 5], F32))
        self.tmp = E(nc.sbuf_tensor(f"{tag}_tmp", [128, maxw], F32))
        if psum is None:
            self.ps = E(nc.psum_tensor(f"{tag}_ps", [1, maxw], F32))
            self.pq = E(nc.psum_tensor(f"{tag}_pq", [1, maxw], F32))
            self.pb = E(nc.psum_tensor(f"{tag}_pb", [128, 2, maxw], F32))
        else:
            self.ps, self.pq, self.pb = psum
        self.prev_v = 0
        self.prev_t = 0

    def emit(self, xf, xln_out, w, wout=None, width=512, gb=None):
        p, nc, cst = self.p, self.p.nc, self.cst
        wd = width
        p.wait_map(nc.scalar, w)
        p.wait(nc.scalar, "T", self.prev_t)
        cv = p.inc(nc.scalar.activation(self.xbf[:, :, :wd], xf, AF.Copy), "S")
        p.wait(nc.scalar, "S", cv)
        sv = p.inc(nc.scalar.activation(self.sq[:, :, :wd], self.xbf[:, :, :wd],
                                        AF.Square), "S")
        p.wait(nc.tensor, "S", cv)
        p.wait(nc.tensor, "V", self.prev_v)
        for c in range(NFT):
            nc.tensor.matmul(self.ps[:, :wd], cst.ones_col_bf[:],
                             self.xbf[:, c, :wd], start=(c == 0), stop=(c == 7))
        nc.tensor.wait_ge(p.sem["S"], sv)
        for c in range(NFT):
            ins = nc.tensor.matmul(self.pq[:, :wd], cst.ones_col_bf[:],
                                   self.sq[:, c, :wd], start=(c == 0), stop=(c == 7))
        tstat = p.inc(ins, "T")
        mw = self.maxw
        mu = self.rows[0:1, 0:wd]
        inv = self.rows[0:1, mw:mw + wd]
        mui = self.rows[0:1, 2 * mw:2 * mw + wd]
        var = self.rows[0:1, 3 * mw:3 * mw + wd]
        t2 = self.rows[0:1, 4 * mw:4 * mw + wd]
        p.wait(nc.vector, "T", tstat)
        p.inc(nc.vector.tensor_scalar_mul(mu, self.ps[:, :wd], 1.0 / D), "V")
        p.inc(nc.vector.tensor_scalar_mul(var, self.pq[:, :wd], 1.0 / D), "V")
        p.wait(nc.vector, "V")
        p.inc(nc.vector.tensor_mul(t2, mu, mu), "V")
        p.wait(nc.vector, "V")
        p.inc(nc.vector.tensor_sub(var, var, t2), "V")
        p.wait(nc.vector, "V")
        p.inc(nc.vector.tensor_scalar_add(var, var, 1e-5), "V")
        p.wait(nc.vector, "V")
        rv = p.inc(nc.vector.reciprocal(var, var), "V")
        p.wait(nc.scalar, "V", rv)
        iv = p.inc(nc.scalar.activation(inv, var, AF.Sqrt), "S")
        p.wait(nc.vector, "S", iv)
        mv = p.inc(nc.vector.tensor_mul(mui, mu, inv), "V")
        p.wait(nc.tensor, "V", mv)
        nc.tensor.matmul(self.pb[:, 0, :wd], cst.ones_row_f32[:], inv,
                         start=True, stop=True)
        tb = p.inc(nc.tensor.matmul(self.pb[:, 1, :wd], cst.ones_row_f32[:], mui,
                                    start=True, stop=True), "T")
        self.prev_t = tb
        p.wait(nc.vector, "T", tb)
        p.wait(nc.vector, "S", getattr(self, "prev_s", 0))
        if wout:
            p.wait_map(nc.vector, wout)
        if gb is None:
            for c in range(NFT):
                if c:
                    p.wait(nc.vector, "V")
                p.inc(nc.vector.tensor_mul(self.tmp[:, :wd], xf[:, c, :wd],
                                           self.pb[:, 0, :wd]), "V")
                p.wait(nc.vector, "V")
                p.inc(nc.vector.tensor_sub(xln_out[:, c, :wd], self.tmp[:, :wd],
                                           self.pb[:, 1, :wd]), "V")
            self.prev_v = p.val("V")
            return self.prev_v
        g_sb, b_sb = gb
        for c in range(NFT):
            if c:
                p.wait(nc.vector, "V")
                p.wait(nc.vector, "S")
            p.inc(nc.vector.tensor_mul(self.tmp[:, :wd], xf[:, c, :wd],
                                       self.pb[:, 0, :wd]), "V")
            p.wait(nc.vector, "V")
            p.inc(nc.vector.tensor_sub(self.tmp[:, :wd], self.tmp[:, :wd],
                                       self.pb[:, 1, :wd]), "V")
            p.wait(nc.vector, "V")
            p.inc(nc.vector.tensor_scalar(xln_out[:, c, :wd], self.tmp[:, :wd],
                                          g_sb[:, c:c + 1], b_sb[:, c:c + 1],
                                          ALU.mult, ALU.add), "V")
        self.prev_v = p.val("V")
        self.prev_s = p.val("S")
        return p.val("V")


PASSES = [list(range(0, 4)), list(range(4, 8)), list(range(8, 12)),
          list(range(12, 15)), list(range(15, 18))]


def stage_l0ffn(p, E, cst, src, dst, d_xln, d_acc, d_grows, w_up, w_down, w_rw, parts=15):
    nc = p.nc
    srcv, dstv = fmv(src), fmv(dst)
    xlnv = fmv(d_xln)
    accv = fmv(d_acc)

    with contextlib.ExitStack() as ph:
        Ep = ph.enter_context
        ln = LNTile(p, Ep, cst, "s1ln")
        xf = Ep(nc.sbuf_tensor("s1_xf", [128, 2, 8, 512], F32))
        xln_t = Ep(nc.sbuf_tensor("s1_xlt", [128, 2, 8, 512], BF16))
        rw_sb = Ep(nc.sbuf_tensor("s1_rw", [128, 8, 16], BF16))
        gate_em = Ep(nc.sbuf_tensor("s1_gem", [16, NT], F32))
        rbuf = Ep(nc.sbuf_tensor("s1_rb", [128, 6, 64], F32))
        prt = Ep(nc.psum_tensor("s1_prt", [128, 64], F32))
        pgt = Ep(nc.psum_tensor("s1_pgt", [16, 512], F32))
        p.inc(nc.gpsimd.dma_start(
            rw_sb[:], w_rw.rearrange("(c p) n -> p c n", p=128)), "G", 16)
        rw_ok = p.val("G")
        xf_free = [0, 0]
        xlt_busy = [{}, {}]
        prev_exp = 0
        prev_gcp = 0
        for it in range(NT // 512):
            sl = it % 2
            t0 = it * 512
            p.wait(nc.sync, "V", xf_free[sl])
            ld = p.inc(nc.sync.dma_start(xf[:, sl], srcv[:, :, t0:t0 + 512]), "Sy", 16)
            vdone = ln.emit(xf[:, sl], xln_t[:, sl], {"Sy": ld}, wout=xlt_busy[sl])
            xf_free[sl] = vdone
            p.wait(nc.sync, "V", vdone)
            xst = p.inc(
                nc.sync.dma_start(xlnv[:, :, t0:t0 + 512], xln_t[:, sl]), "Sy", 16)
            if not (parts & 8):
                xlt_busy[sl] = {"Sy": xst}
                continue
            p.wait(nc.tensor, "G", rw_ok)
            p.wait(nc.tensor, "V", vdone)
            p.wait(nc.tensor, "S", prev_exp)
            for s in range(4):
                for c in range(NFT):
                    ins = nc.tensor.matmul(
                        prt[:, s * 16:s * 16 + 16],
                        xln_t[:, sl, c, s * 128:(s + 1) * 128],
                        rw_sb[:, c, :], start=(c == 0 and s == 0),
                        stop=(c == 7 and s == 3))
            trt = p.inc(ins, "T")
            xlt_busy[sl] = {"T": trt, "Sy": xst}
            ex = rbuf[:, 0, :]
            sm = rbuf[:, 1, 0:4].rearrange("p (g o) -> p g o", o=1)
            rr = rbuf[:, 1, 4:8].rearrange("p (g o) -> p g o", o=1)
            probs = rbuf[:, 2, :]
            pr3 = probs.rearrange("p (g e) -> p g e", e=16)
            work = rbuf[:, 3, :]
            wk3 = work.rearrange("p (g e) -> p g e", e=16)
            m4 = rbuf[:, 4, 0:4].rearrange("p (g o) -> p g o", o=1)
            gt = rbuf[:, 5, :]
            gt3 = gt.rearrange("p (g e) -> p g e", e=16)
            p.wait(nc.scalar, "T", trt)
            ev = p.inc(nc.scalar.activation(ex, prt[:], AF.Exp), "S")
            prev_exp = ev
            p.wait(nc.vector, "S", ev)
            ex3 = ex.rearrange("p (g e) -> p g e", e=16)
            p.inc(nc.vector.reduce_sum(sm, ex3, axis=AX.X), "V")
            p.wait(nc.vector, "V")
            p.inc(nc.vector.reciprocal(rr, sm), "V")
            p.wait(nc.vector, "V")
            p.inc(nc.vector.tensor_tensor(pr3, ex3,
                                          rr.broadcast_to([128, 4, 16]),
                                          ALU.mult), "V")
            p.wait(nc.vector, "V")
            p.inc(nc.vector.tensor_copy(work, probs), "V")
            for r in range(3):
                p.wait(nc.vector, "V")
                p.inc(nc.vector.reduce_max(m4, wk3, axis=AX.X), "V")
                p.wait(nc.vector, "V")
                p.inc(nc.vector.tensor_tensor(
                    gt3, wk3, m4.broadcast_to([128, 4, 16]), ALU.is_ge), "V")
                p.wait(nc.vector, "V")
                p.inc(nc.vector.scalar_tensor_tensor(
                    work, gt, -4.0, work, ALU.mult, ALU.add), "V")
            p.wait(nc.vector, "V")
            p.inc(nc.vector.reduce_max(m4, wk3, axis=AX.X), "V")
            p.wait(nc.vector, "V")
            p.inc(nc.vector.tensor_tensor(
                gt3, pr3, m4.broadcast_to([128, 4, 16]), ALU.is_ge), "V")
            p.wait(nc.vector, "V")
            gv = p.inc(nc.vector.tensor_mul(gt, gt, probs), "V")
            p.wait(nc.tensor, "V", gv)
            p.wait(nc.tensor, "S", prev_gcp)
            for s in range(4):
                ins = nc.tensor.transpose(pgt[:, s * 128:(s + 1) * 128],
                                          gt3[:, s, :], cst.idn_f32[:])
            tg = p.inc(ins, "T")
            p.wait(nc.scalar, "T", tg)
            prev_gcp = p.inc(nc.scalar.copy(gate_em[:, t0:t0 + 512], pgt[:]), "S")
        if not (parts & 8):
            return
        p.wait(nc.sync, "S", prev_gcp)
        for e in range(16):
            p.inc(nc.sync.dma_start(d_grows[e:e + 1, :], gate_em[e:e + 1, :]),
                  "Sy", 16)
        grows_ok = p.val("Sy")

    if not (parts & 2):
        return
    p.barrier()
    with contextlib.ExitStack() as ph:
        Ep = ph.enter_context
        wub = Ep(nc.sbuf_tensor("s1_wu", [128, 2, 4, 8, 512], BF16))
        wdb = Ep(nc.sbuf_tensor("s1_wd", [128, 2, 4, 4, 1024], BF16))
        xq = Ep(nc.sbuf_tensor("s1_xq", [128, 2, 8, 256], BF16))
        grow = Ep(nc.sbuf_tensor("s1_gr", [1, 2, 16 * 256], BF16))
        act = Ep(nc.sbuf_tensor("s1_act", [128, 4, 256], BF16))
        geb = Ep(nc.sbuf_tensor("s1_geb", [128, 4, 256], BF16))
        stg = Ep(nc.sbuf_tensor("s1_stg", [128, 2, 8, 256], F32))
        pup = Ep(nc.psum_tensor("s1_pup", [128, 4, 256], F32))
        pout = Ep(nc.psum_tensor("s1_pout", [128, 8, 256], F32))

        NTILE = NT // 256
        wslot_free = [0, 0]
        xq_free = [0, 0]
        grow_free = [0, 0]
        stg_free = [0, 0]
        wready = [0, 0]

        def load_pass(ip, slot):
            exs = PASSES[ip]
            p.wait(nc.gpsimd, "T", wslot_free[slot])
            for j, e in enumerate(exs):
                p.inc(nc.gpsimd.dma_start(
                    wub[:, slot, j],
                    w_up[e].rearrange("(c p) n -> p c n", p=128)), "G", 16)
                p.inc(nc.gpsimd.dma_start(
                    wdb[:, slot, j],
                    w_down[e].rearrange("(c p) n -> p c n", p=128)), "G", 16)
            wready[slot] = p.val("G")

        load_pass(0, 0)
        for ip, exs in enumerate(PASSES):
            slot = ip % 2
            for it in range(NTILE):
                ts = it % 2
                t0 = it * 256
                p.wait(nc.gpsimd, "T", xq_free[ts])
                p.wait(nc.gpsimd, "Sy", grows_ok)
                lx = p.inc(nc.gpsimd.dma_start(
                    xq[:, ts], xlnv[:, :, t0:t0 + 256]), "G", 16)
                p.wait(nc.gpsimd, "T", grow_free[ts])
                lg = p.inc(nc.gpsimd.dma_start(
                    grow[:, ts], d_grows[:, t0:t0 + 256]), "G", 16)
                if it == 0 and ip + 1 < len(PASSES):
                    load_pass(ip + 1, (ip + 1) % 2)
                p.wait(nc.tensor, "G", lg)
                p.wait(nc.tensor, "S", p.val("S"))
                ngate = 0
                for j, e in enumerate(exs):
                    if e < 2:
                        continue
                    sl4 = ngate
                    ngate += 1
                    tv = p.inc(nc.tensor.matmul(
                        pout[:, sl4, :], cst.ones_row_bf[:],
                        grow[0:1, ts, (e - 2) * 256:(e - 1) * 256],
                        start=(sl4 % 2 == 0), stop=True), "T")
                    p.wait(nc.scalar, "T", tv)
                    p.inc(nc.scalar.copy(geb[:, j, :], pout[:, sl4, :]), "S")
                geb_ok = p.val("S")
                p.wait(nc.tensor, "G", max(wready[slot], lx))
                first_bank = [True] * 4
                for j, e in enumerate(exs):
                    p.wait(nc.tensor, "S", p.val("S"))
                    for m in range(4):
                        for c in range(NFT):
                            ins = nc.tensor.matmul(
                                pup[:, m, :],
                                wub[:, slot, j, c, m * 128:(m + 1) * 128],
                                xq[:, ts, c, :],
                                start=(c == 0 and m % 2 == 0), stop=False)
                    tu = p.inc(ins, "T")
                    p.wait(nc.scalar, "T", tu)
                    ga = p.inc(nc.scalar.activation(act[:], pup[:], AF.Gelu), "S")
                    if e >= 2:
                        p.wait(nc.gpsimd, "S", max(ga, geb_ok))
                        for ec in range(4):
                            ins = nc.gpsimd.tensor_mul(
                                act[:, ec, :], act[:, ec, :], geb[:, j, :])
                        gm = p.inc(ins, "P")
                        p.wait(nc.tensor, "P", gm)
                    else:
                        p.wait(nc.tensor, "S", ga)
                    for m in range(NFT):
                        bank = m // 2
                        st = first_bank[bank]
                        for ec in range(4):
                            ins = nc.tensor.matmul(
                                pout[:, m, :],
                                wdb[:, slot, j, ec, m * 128:(m + 1) * 128],
                                act[:, ec, :], start=(st and ec == 0), stop=False)
                        first_bank[bank] = False
                    td = p.inc(ins, "T")
                if it == NTILE - 1:
                    wslot_free[slot] = p.val("T")
                xq_free[ts] = p.val("T")
                grow_free[ts] = p.val("T")
                p.wait(nc.scalar, "T", p.val("T"))
                p.wait(nc.scalar, "G", stg_free[ts])
                ev = p.inc(nc.scalar.copy(stg[:, ts], pout[:]), "S")
                p.wait(nc.gpsimd, "S", ev)
                op = ALU.add if ip > 0 else ALU.bypass
                stg_free[ts] = p.inc(
                    nc.gpsimd.dma_start(accv[:, :, t0:t0 + 256], stg[:, ts],
                                        accum_op=op), "G", 16)
        ph2_done = {"G": p.val("G")}

    if not (parts & 4):
        return
    p.barrier()
    with contextlib.ExitStack() as ph:
        Ep = ph.enter_context
        xa = Ep(nc.sbuf_tensor("s1_xa", [128, 2, 8, 512], F32))
        aa = Ep(nc.sbuf_tensor("s1_aa", [128, 2, 8, 512], F32))
        yy = Ep(nc.sbuf_tensor("s1_yy", [128, 2, 8, 512], F32))
        free_v = [0, 0]
        yst = [0, 0]
        for it in range(NT // 512):
            sl = it % 2
            t0 = it * 512
            p.wait(nc.sync, "V", free_v[sl])
            p.wait(nc.sync, "G", ph2_done["G"])
            lx = p.inc(nc.sync.dma_start(xa[:, sl], srcv[:, :, t0:t0 + 512]),
                       "Sy", 16)
            la = p.inc(nc.sync.dma_start(aa[:, sl], accv[:, :, t0:t0 + 512]),
                       "Sy", 16)
            p.wait(nc.vector, "Sy", la)
            p.wait(nc.vector, "Sy", yst[sl])
            for c in range(NFT):
                ins = nc.vector.scalar_tensor_tensor(
                    yy[:, sl, c, :], aa[:, sl, c, :], 0.5, xa[:, sl, c, :],
                    ALU.mult, ALU.add)
            vv = p.inc(ins, "V")
            free_v[sl] = vv
            p.wait(nc.sync, "V", vv)
            yst[sl] = p.inc(nc.sync.dma_start(dstv[:, :, t0:t0 + 512],
                                              yy[:, sl]), "Sy", 16)


def build_s1(debug=False, parts=15):
    nc = bass.Bass("TRN2", target_bir_lowering=False)
    ctx = contextlib.ExitStack()
    with ctx:
        p = PB(nc, ctx)
        E = ctx.enter_context
        src = nc.dram_tensor("x_fm", [D, NT], F32, kind="ExternalInput").ap()
        dst = nc.dram_tensor("y_fm", [D, NT], F32, kind="ExternalOutput").ap()
        w_up = nc.dram_tensor("l0f_up", *WEIGHT_SPECS["l0f_up"],
                              kind="ExternalInput").ap()
        w_down = nc.dram_tensor("l0f_down", *WEIGHT_SPECS["l0f_down"],
                                kind="ExternalInput").ap()
        w_rw = nc.dram_tensor("l0f_rw", *WEIGHT_SPECS["l0f_rw"],
                              kind="ExternalInput").ap()
        kd = "ExternalOutput" if debug else "Internal"
        d_xln = nc.dram_tensor("d_xln", [D, NT], BF16, kind=kd).ap()
        d_acc = nc.dram_tensor("d_acc", [D, NT], F32, kind=kd).ap()
        d_grows = nc.dram_tensor("d_grows", [16, NT], F32, kind=kd).ap()
        cst = Consts(p, E, ["ones_col_f32", "ones_col_bf", "ones_row_f32",
                            "ones_row_bf", "idn_f32"])
        cst.barrier(p)
        stage_l0ffn(p, E, cst, src, dst, d_xln, d_acc, d_grows,
                    w_up, w_down, w_rw, parts=parts)
        p.end()
    return nc




def stage_l0attn(p, E, cst, src, dst, w_qT, w_kvdT, w_kuT, w_vuT, w_oT,
                 parts=31):
    nc = p.nc
    srcv, dstv = fmv(src), fmv(dst)
    wqv = w_qT.rearrange("(c p) n -> p c n", p=128)
    wov = w_oT.rearrange("(c p) n -> p c n", p=128)
    HT = T // 2

    with contextlib.ExitStack() as ph:
        Ep = ph.enter_context
        xf = Ep(nc.sbuf_tensor("s2_xf", [128, 8, 512], F32))
        xlnb = Ep(nc.sbuf_tensor("s2_xln", [128, 8, T], BF16))
        wkvd = Ep(nc.sbuf_tensor("s2_wkvd", [128, 8, 64], BF16))
        wku = Ep(nc.sbuf_tensor("s2_wku", [64, 16, 128], BF16))
        wvu = Ep(nc.sbuf_tensor("s2_wvu", [64, 8, 256], BF16))
        wq = Ep(nc.sbuf_tensor("s2_wq", [128, 2, 8, 256], BF16))
        wo = Ep(nc.sbuf_tensor("s2_wo", [128, 16, 512], BF16))
        latb = Ep(nc.sbuf_tensor("s2_lat", [64, T], BF16))
        qh = Ep(nc.sbuf_tensor("s2_qh", [128, 2, HT], BF16))
        kh = Ep(nc.sbuf_tensor("s2_kh", [128, 2, T], BF16))
        vh = Ep(nc.sbuf_tensor("s2_vh", [128, 16, 256], BF16))
        attn = Ep(nc.sbuf_tensor("s2_attn", [128, 4, 512], BF16))
        atnT = Ep(nc.sbuf_tensor("s2_atnT", [128, 16, 128], BF16))
        aout = Ep(nc.sbuf_tensor("s2_aout", [128, 16, HT], BF16))
        rr = Ep(nc.sbuf_tensor("s2_rr", [128, 2], F32))
        xr = Ep(nc.sbuf_tensor("s2_xr", [128, 4, 512], F32))
        yr = Ep(nc.sbuf_tensor("s2_yr", [128, 4, 512], F32))
        pzA = Ep(nc.psum_tensor("s2_pzA", [128, 6, 512], F32))
        pzB = Ep(nc.psum_tensor("s2_pzB", [128, 16, 128], BF16))
        ln = LNTile(p, Ep, cst, "s2ln",
                    psum=(pzA[0:1, 0, :], pzA[0:1, 1, :], pzA[:, 2:4, :]))

        p.inc(nc.gpsimd.dma_start(
            wkvd[:], w_kvdT.rearrange("(c p) n -> p c n", p=128)), "G", 16)
        p.inc(nc.gpsimd.dma_start(
            wku[:], w_kuT.rearrange("k (s n) -> k s n", n=128)), "G", 16)
        p.inc(nc.gpsimd.dma_start(
            wvu[:], w_vuT.rearrange("k (s n) -> k s n", n=256)), "G", 16)
        wstat = p.val("G")

        for b in range(BPC):
            toff = b * T
            for tt in range(4):
                t0 = tt * 512
                p.wait(nc.sync, "V", ln.prev_v)
                ld = p.inc(nc.sync.dma_start(
                    xf[:], srcv[:, :, toff + t0:toff + t0 + 512]), "Sy", 16)
                ln.emit(xf[:], xlnb[:, :, t0:t0 + 512], {"Sy": ld})
            lnv = ln.prev_v
            if not (parts & 2):
                continue
            p.wait(nc.tensor, "G", wstat)
            p.wait(nc.tensor, "V", lnv)
            p.wait(nc.tensor, "S", p.val("S"))
            for tt in range(4):
                p.wait(nc.tensor, "S", p.val("S"))
                for c in range(NFT):
                    ins = nc.tensor.matmul(
                        pzA[0:64, 2, :], wkvd[:, c, :],
                        xlnb[:, c, tt * 512:(tt + 1) * 512],
                        start=(c == 0), stop=(c == 7))
                tl = p.inc(ins, "T")
                p.wait(nc.scalar, "T", tl)
                p.inc(nc.scalar.copy(latb[:, tt * 512:(tt + 1) * 512],
                                     pzA[0:64, 2, :]), "S")
            lat_ok = p.val("S")

            for th in range(2 if (parts & 4) else 0):
                hoff = th * HT
                wq_ok = [0, 0]
                wq_free = [0, 0]

                def load_wq(h):
                    sl = h % 2
                    p.wait(nc.gpsimd, "T", wq_free[sl])
                    p.inc(nc.gpsimd.dma_start(
                        wq[:, sl], wqv[:, :, h * 256:(h + 1) * 256]), "G", 16)
                    wq_ok[sl] = p.val("G")

                load_wq(0)
                for h in range(8):
                    hs = h % 2
                    if h < 7:
                        load_wq(h + 1)
                    p.wait(nc.tensor, "G", wq_ok[hs])
                    p.wait(nc.tensor, "S", lat_ok)
                    for tt in range(2):
                        p.wait(nc.tensor, "S", p.val("S"))
                        for m in range(2):
                            for c in range(NFT):
                                ins = nc.tensor.matmul(
                                    pzA[:, m, :],
                                    wq[:, hs, c, m * 128:(m + 1) * 128],
                                    xlnb[:, c,
                                         hoff + tt * 512:hoff + (tt + 1) * 512],
                                    start=(c == 0), stop=(c == 7))
                        tq = p.inc(ins, "T")
                        p.wait(nc.scalar, "T", tq)
                        p.inc(nc.scalar.copy(qh[:, :, tt * 512:(tt + 1) * 512],
                                             pzA[:, 0:2, :]), "S")
                    wq_free[hs] = p.val("T")
                    for tt in range(4):
                        p.wait(nc.tensor, "S", p.val("S"))
                        for m in range(2):
                            ins = nc.tensor.matmul(
                                pzA[:, m, :], wku[:, h * 2 + m, :],
                                latb[:, tt * 512:(tt + 1) * 512],
                                start=True, stop=True)
                        tk = p.inc(ins, "T")
                        p.wait(nc.scalar, "T", tk)
                        p.inc(nc.scalar.copy(kh[:, :, tt * 512:(tt + 1) * 512],
                                             pzA[:, 0:2, :]), "S")
                    for g in range(4):
                        p.wait(nc.tensor, "S", p.val("S"))
                        for t4 in range(4):
                            tt = g * 4 + t4
                            ins = nc.tensor.matmul(
                                pzA[:, t4, 0:256],
                                latb[:, tt * 128:(tt + 1) * 128],
                                wvu[:, h, :], start=True, stop=True)
                        tv = p.inc(ins, "T")
                        p.wait(nc.scalar, "T", tv)
                        p.inc(nc.scalar.copy(vh[:, g * 4:g * 4 + 4, :],
                                             pzA[:, 0:4, 0:256]), "S")
                    for qt in range(8 if (parts & 8) else 0):
                        q0 = qt * 128
                        p.wait(nc.tensor, "S", p.val("S"))
                        for n4 in range(4):
                            for c2 in range(2):
                                ins = nc.tensor.matmul(
                                    pzA[:, n4, :], qh[:, c2, q0:q0 + 128],
                                    kh[:, c2, n4 * 512:(n4 + 1) * 512],
                                    start=(c2 == 0), stop=(c2 == 1))
                        ts_ = p.inc(ins, "T")
                        p.wait(nc.vector, "T", ts_)
                        p.wait(nc.vector, "S", p.val("S"))
                        nm = p.inc(nc.vector.reduce_max(
                            rr[:, 0:1], pzA[:, 0:4, :], axis=AX.XY,
                            negate=True), "V")
                        p.wait(nc.vector, "V", nm)
                        nm2 = p.inc(nc.vector.tensor_scalar_mul(
                            rr[:, 0:1], rr[:, 0:1], 1.0 / 16.0), "V")
                        p.wait(nc.scalar, "V", nm2)
                        ex = p.inc(nc.scalar.activation(
                            attn[:], pzA[:, 0:4, :], AF.Exp, bias=rr[:, 0:1],
                            scale=1.0 / 16.0, accum_out=rr[:, 1:2]), "S")
                        p.wait(nc.vector, "S", ex)
                        rv = p.inc(nc.vector.reciprocal(rr[:, 1:2],
                                                        rr[:, 1:2]), "V")
                        p.wait(nc.scalar, "V", rv)
                        an = p.inc(nc.scalar.mul(attn[:], attn[:],
                                                 rr[:, 1:2]), "S")
                        p.wait(nc.tensor, "S", an)
                        av = attn[:].rearrange("p a (g w) -> p (a g) w", w=128)
                        for t in range(16):
                            ins = nc.tensor.transpose(pzB[:, t, :], av[:, t, :],
                                                      cst.idn_bf[:])
                        tt_ = p.inc(ins, "T")
                        p.wait(nc.scalar, "T", tt_)
                        at = p.inc(nc.scalar.copy(atnT[:], pzB[:]), "S")
                        p.wait(nc.tensor, "S", at)
                        for m in range(2):
                            for t in range(16):
                                ins = nc.tensor.matmul(
                                    pzA[:, 4, m * 128:(m + 1) * 128],
                                    vh[:, t, m * 128:(m + 1) * 128],
                                    atnT[:, t, :],
                                    start=(m == 0 and t == 0), stop=False)
                        to = p.inc(ins, "T")
                        p.wait(nc.scalar, "T", to)
                        p.inc(nc.scalar.copy(
                            aout[:, h * 2:h * 2 + 2, q0:q0 + 128],
                            pzA[:, 4, 0:256].rearrange("p (m w) -> p m w",
                                                        w=128)), "S")
                for half in range(2 if (parts & 16) else 0):
                    p.wait(nc.gpsimd, "T", p.val("T"))
                    p.wait(nc.gpsimd, "S", p.val("S"))
                    lw = p.inc(nc.gpsimd.dma_start(
                        wo[:], wov[:, :, half * 512:(half + 1) * 512]), "G", 16)
                    for qt4 in range(2):
                        t0 = qt4 * 512
                        p.wait(nc.tensor, "G", lw)
                        p.wait(nc.tensor, "S", p.val("S"))
                        p.wait(nc.tensor, "V", p.val("V"))
                        for m in range(4):
                            for kc in range(16):
                                ins = nc.tensor.matmul(
                                    pzA[:, m, :], wo[:, kc, m * 128:(m + 1) * 128],
                                    aout[:, kc, t0:t0 + 512],
                                    start=(kc == 0), stop=(kc == 15))
                        tw = p.inc(ins, "T")
                        lx = p.inc(nc.sync.dma_start(
                            xr[:], srcv[:, half * 4:half * 4 + 4,
                                        toff + hoff + t0:toff + hoff + t0 + 512]),
                            "Sy", 16)
                        p.wait(nc.vector, "T", tw)
                        p.wait(nc.vector, "Sy", lx)
                        for m in range(4):
                            ins = nc.vector.tensor_add(
                                yr[:, m, :], pzA[:, m, :], xr[:, m, :])
                        vv = p.inc(ins, "V")
                        p.wait(nc.sync, "V", vv)
                        st = p.inc(nc.sync.dma_start(
                            dstv[:, half * 4:half * 4 + 4,
                                 toff + hoff + t0:toff + hoff + t0 + 512],
                            yr[:]), "Sy", 16)
                        p.wait(nc.vector, "Sy", st)
                        p.wait(nc.sync, "Sy", st)
                p.barrier()


def build_s2(parts=31):
    nc = bass.Bass("TRN2", target_bir_lowering=False)
    ctx = contextlib.ExitStack()
    with ctx:
        p = PB(nc, ctx)
        E = ctx.enter_context
        src = nc.dram_tensor("x_fm", [D, NT], F32, kind="ExternalInput").ap()
        dst = nc.dram_tensor("y_fm", [D, NT], F32, kind="ExternalOutput").ap()
        ws = {k: nc.dram_tensor(k, *WEIGHT_SPECS[k], kind="ExternalInput").ap()
              for k in ["l0a_wqT", "l0a_wkvdT", "l0a_wkuT", "l0a_wvuT", "l0a_woT"]}
        cst = Consts(p, E, ["ones_col_bf", "ones_row_f32", "idn_bf"])
        cst.barrier(p)
        stage_l0attn(p, E, cst, src, dst, ws["l0a_wqT"], ws["l0a_wkvdT"],
                     ws["l0a_wkuT"], ws["l0a_wvuT"], ws["l0a_woT"], parts=parts)
        p.end()
    return nc




def stage_l1attn(p, E, cst, src, dst, w4T, b4, wabT, bab, woT, bo_fm,
                 dbg=None):
    nc = p.nc
    srcv, dstv = fmv(src), fmv(dst)

    with contextlib.ExitStack() as ph:
        Ep = ph.enter_context
        xf = Ep(nc.sbuf_tensor("s5_xf", [128, 8, 512], F32))
        xlnb = Ep(nc.sbuf_tensor("s5_xln", [128, 8, T], BF16))
        w4 = Ep(nc.sbuf_tensor("s5_w4", [128, 8, 512], BF16))
        wab = Ep(nc.sbuf_tensor("s5_wab", [128, 8, 2], BF16))
        wol = Ep(nc.sbuf_tensor("s5_wol", [128, 8, 128], BF16))
        bo = Ep(nc.sbuf_tensor("s5_bo", [128, 8], F32))
        rows = Ep(nc.sbuf_tensor("s5_rows", [1, 1024], F32))
        b4B = Ep(nc.sbuf_tensor("s5_b4B", [128, 512], F32))
        babB = Ep(nc.sbuf_tensor("s5_babB", [128, 2], F32))
        projf = Ep(nc.sbuf_tensor("s5_projf", [128, 512], F32))
        abf = Ep(nc.sbuf_tensor("s5_abf", [128, 16, 2], F32))
        cols = Ep(nc.sbuf_tensor("s5_cols", [128, 12], F32))
        Ktm = Ep(nc.sbuf_tensor("s5_Ktm", [128, 128], F32))
        Kfm = Ep(nc.sbuf_tensor("s5_Kfm", [128, 128], F32))
        Qg = Ep(nc.sbuf_tensor("s5_Qg", [128, 128], F32))
        Qgf = Ep(nc.sbuf_tensor("s5_Qgf", [128, 128], F32))
        rgB = Ep(nc.sbuf_tensor("s5_rgB", [128, 128], F32))
        agB = Ep(nc.sbuf_tensor("s5_agB", [128, 128], F32))
        NbB = Ep(nc.sbuf_tensor("s5_NbB", [128, 128], F32))
        ETz = Ep(nc.sbuf_tensor("s5_ETz", [128, 128], F32))
        E2T = Ep(nc.sbuf_tensor("s5_E2T", [128, 128], F32))
        Qfm = Ep(nc.sbuf_tensor("s5_Qfm", [128, 128], F32))
        JT = Ep(nc.sbuf_tensor("s5_JT", [128, 128], F32))
        Pm = Ep(nc.sbuf_tensor("s5_Pm", [128, 128], F32))
        PT = Ep(nc.sbuf_tensor("s5_PT", [128, 128], F32))
        Sm = Ep(nc.sbuf_tensor("s5_Sm", [128, 128], F32))
        STm = Ep(nc.sbuf_tensor("s5_STm", [128, 128], F32))
        Rm = Ep(nc.sbuf_tensor("s5_Rm", [128, 128], F32))
        Wtm = Ep(nc.sbuf_tensor("s5_Wtm", [128, 128], F32))
        Wfm = Ep(nc.sbuf_tensor("s5_Wfm", [128, 128], F32))
        Wg = Ep(nc.sbuf_tensor("s5_Wg", [128, 128], F32))
        Wig = Ep(nc.sbuf_tensor("s5_Wig", [128, 128], F32))
        PTm = Ep(nc.sbuf_tensor("s5_PTm", [128, 128], F32))
        Zs = Ep(nc.sbuf_tensor("s5_Z", [128, 128], F32))
        ZT = Ep(nc.sbuf_tensor("s5_ZT", [128, 128], F32))
        t1 = Ep(nc.sbuf_tensor("s5_t1", [128, 128], F32))
        t2 = Ep(nc.sbuf_tensor("s5_t2", [128, 128], F32))
        ogb = Ep(nc.sbuf_tensor("s5_ogb", [128, 128], BF16))
        ogf = Ep(nc.sbuf_tensor("s5_ogf", [128, 128], BF16))
        xr = Ep(nc.sbuf_tensor("s5_xr", [128, 8, 128], F32))
        yr = Ep(nc.sbuf_tensor("s5_yr", [128, 8, 128], F32))
        pP = Ep(nc.psum_tensor("s5_pP", [128, 512], F32))
        pab = Ep(nc.psum_tensor("s5_pab", [128, 4, 2], F32))
        pM = Ep(nc.psum_tensor("s5_pM", [128, 4, 128], F32))
        pN = Ep(nc.psum_tensor("s5_pN", [128, 4, 128], F32))
        pC = Ep(nc.psum_tensor("s5_pC", [128, 4, 128], F32))
        pR = Ep(nc.psum_tensor("s5_pR", [128, 8, 128], F32))
        pBb = Ep(nc.psum_tensor("s5_pBb", [128, 1, 128], BF16))
        pMf = pM[:, :, :].rearrange("p a b -> p (a b)")
        pRv = pR[:, :, :].rearrange("p a b -> p (a b)").rearrange(
            "p (a w) -> p a w", w=512)
        ln = LNTile(p, Ep, cst, "s5ln",
                    psum=(pP[0:1, 0:512], pMf[0:1, 0:512], pRv))

        p.inc(nc.gpsimd.dma_start(
            w4[:], w4T.rearrange("(c p) n -> p c n", p=128)), "G", 16)
        p.inc(nc.gpsimd.dma_start(
            wab[:], wabT.rearrange("(c p) n -> p c n", p=128)), "G", 16)
        p.inc(nc.gpsimd.dma_start(
            wol[:], woT.rearrange("k (m n) -> k m n", n=128)), "G", 16)
        p.inc(nc.gpsimd.dma_start(bo[:], bo_fm), "G", 16)
        p.inc(nc.sync.dma_start(rows[0:1, 0:512], b4), "Sy", 16)
        p.inc(nc.sync.dma_start(rows[0:1, 512:514], bab), "Sy", 16)
        p.wait(nc.tensor, "Sy", p.val("Sy"))
        p.wait(nc.tensor, "G", p.val("G"))
        nc.tensor.matmul(pP[:, 0:512], cst.ones_row_f32[:], rows[0:1, 0:512],
                         start=True, stop=True)
        tb = p.inc(nc.tensor.matmul(pab[:, 0, :], cst.ones_row_f32[:],
                                    rows[0:1, 512:514], start=True, stop=True),
                   "T")
        p.wait(nc.scalar, "T", tb)
        p.inc(nc.scalar.copy(b4B[:], pP[:, 0:512]), "S")
        sb0 = p.inc(nc.scalar.copy(babB[:], pab[:, 0, :]), "S")
        p.wait(nc.tensor, "S", sb0)

        for b in range(BPC):
            toff = b * T
            for tt in range(4):
                t0 = tt * 512
                p.wait(nc.sync, "V", ln.prev_v)
                ld = p.inc(nc.sync.dma_start(
                    xf[:], srcv[:, :, toff + t0:toff + t0 + 512]), "Sy", 16)
                ln.emit(xf[:], xlnb[:, :, t0:t0 + 512], {"Sy": ld})
            lnv = ln.prev_v
            p.wait(nc.tensor, "V", lnv)
            p.wait(nc.tensor, "S", p.val("S"))
            for cgrp in range(4):
                p.wait(nc.tensor, "S", p.val("S"))
                p.wait(nc.tensor, "V", p.val("V"))
                for c4 in range(4):
                    ck = cgrp * 4 + c4
                    for c in range(NFT):
                        ins = nc.tensor.matmul(
                            pab[:, c4, :], xlnb[:, c, ck * 128:(ck + 1) * 128],
                            wab[:, c, :], start=(c == 0), stop=(c == 7))
                ta = p.inc(ins, "T")
                p.wait(nc.vector, "T", ta)
                vv = p.inc(nc.vector.tensor_add(
                    abf[:, cgrp * 4:cgrp * 4 + 4, :], pab[:],
                    babB[:].rearrange("p (o n) -> p o n", o=1
                                      ).broadcast_to([128, 4, 2])), "V")
                p.wait(nc.scalar, "V", vv)
                p.inc(nc.scalar.activation(abf[:, cgrp * 4:cgrp * 4 + 4, :],
                                           abf[:, cgrp * 4:cgrp * 4 + 4, :],
                                           AF.Sigmoid), "S")
            ab_ok = p.val("S")
            p.wait(nc.vector, "S", ab_ok)
            p.inc(nc.vector.memzero(Zs[:]), "V")
            zi = p.inc(nc.vector.memzero(ZT[:]), "V")

            for ck in range(16):
                tc = toff + ck * 128

                def TW():
                    p.wait(nc.tensor, "S", p.val("S"))
                    p.wait(nc.tensor, "V", p.val("V"))

                def SW():
                    p.wait(nc.scalar, "T", p.val("T"))
                    p.wait(nc.scalar, "V", p.val("V"))

                def VW():
                    p.wait(nc.vector, "T", p.val("T"))
                    p.wait(nc.vector, "S", p.val("S"))

                TW()
                for c in range(NFT):
                    ins = nc.tensor.matmul(
                        pP[:], xlnb[:, c, ck * 128:(ck + 1) * 128],
                        w4[:, c, :], start=(c == 0), stop=(c == 7))
                p.inc(ins, "T")
                VW()
                p.inc(nc.vector.tensor_add(projf[:], pP[:], b4B[:]), "V")
                SW()
                p.inc(nc.scalar.activation(projf[:, 384:512], projf[:, 384:512],
                                           AF.Sigmoid), "S")
                VW()
                p.inc(nc.vector.tensor_mul(t1[:], projf[:, 128:256],
                                           projf[:, 128:256]), "V")
                p.wait(nc.vector, "V")
                p.inc(nc.vector.reduce_sum(cols[:, 8:9], t1[:], axis=AX.X), "V")
                p.wait(nc.vector, "V")
                p.inc(nc.vector.tensor_scalar_max(cols[:, 8:9], cols[:, 8:9],
                                                  1e-24), "V")
                SW()
                p.inc(nc.scalar.activation(cols[:, 8:9], cols[:, 8:9], AF.Ln), "S")
                p.wait(nc.scalar, "S", p.val("S"))
                p.inc(nc.scalar.activation(cols[:, 0:1], cols[:, 8:9], AF.Exp,
                                           scale=-0.5), "S")
                p.wait(nc.scalar, "S", p.val("S"))
                p.inc(nc.scalar.mul(Ktm[:], projf[:, 128:256], cols[:, 0:1]), "S")
                al = abf[:, ck, 0:1]
                be = abf[:, ck, 1:2]
                p.wait(nc.scalar, "S", p.val("S"))
                p.inc(nc.scalar.activation(cols[:, 7:8], al, AF.Ln), "S")
                TW()
                tcs = p.inc(nc.tensor.matmul(pC[:, 0, 0:1], cst.triu_inc[:],
                                             cols[:, 7:8], start=True, stop=True),
                            "T")
                p.wait(nc.vector, "T", tcs)
                p.inc(nc.vector.tensor_copy(cols[:, 2:3], pC[:, 0, 0:1]), "V")
                p.wait(nc.vector, "V")
                p.inc(nc.vector.tensor_sub(cols[:, 3:4], cols[:, 2:3],
                                           cols[:, 7:8]), "V")
                SW()
                p.inc(nc.scalar.activation(cols[:, 1:2], cols[:, 2:3], AF.Exp),
                      "S")
                p.inc(nc.scalar.activation(cols[:, 9:10], cols[:, 3:4], AF.Exp),
                      "S")
                VW()
                p.inc(nc.vector.tensor_mul(cols[:, 4:5], al, be), "V")
                p.wait(nc.vector, "V")
                p.inc(nc.vector.tensor_scalar_mul(cols[:, 4:5], cols[:, 4:5],
                                                  -1.0), "V")
                p.wait(nc.vector, "V")
                p.inc(nc.vector.tensor_mul(cols[:, 10:11], cols[:, 4:5],
                                           cols[:, 9:10]), "V")
                TW()
                tg1 = p.inc(nc.tensor.matmul(pC[0:1, 1, 0:1], cst.e127[:],
                                             cols[:, 2:3], start=True,
                                             stop=True), "T")
                p.wait(nc.scalar, "T", tg1)
                sg1 = p.inc(nc.scalar.copy(rows[0:1, 770:771],
                                           pC[0:1, 1, 0:1]), "S")
                p.wait(nc.tensor, "S", sg1)
                tgl = p.inc(nc.tensor.matmul(
                    pC[:, 2, 0:1], cst.ones_row_f32[:], rows[0:1, 770:771],
                    start=True, stop=True), "T")
                p.wait(nc.vector, "T", tgl)
                p.inc(nc.vector.tensor_copy(cols[:, 5:6], pC[:, 2, 0:1]), "V")
                p.wait(nc.vector, "V")
                p.inc(nc.vector.tensor_sub(cols[:, 6:7], cols[:, 5:6],
                                           cols[:, 2:3]), "V")
                SW()
                p.inc(nc.scalar.activation(cols[:, 6:7], cols[:, 6:7], AF.Exp),
                      "S")
                p.inc(nc.scalar.activation(cols[:, 11:12], cols[:, 5:6], AF.Exp),
                      "S")
                TW()
                nc.tensor.matmul(pC[0:1, 0, :], cols[:, 2:3], cst.idn_f32[:],
                                 start=True, stop=True)
                nc.tensor.matmul(pC[0:1, 1, :], cols[:, 3:4], cst.idn_f32[:],
                                 start=True, stop=True)
                tr = p.inc(nc.tensor.matmul(pC[0:1, 2, :], cols[:, 4:5],
                                            cst.idn_f32[:], start=True,
                                            stop=True), "T")
                p.wait(nc.scalar, "T", tr)
                p.inc(nc.scalar.copy(rows[0:1, 514:642], pC[0:1, 0, :]), "S")
                p.inc(nc.scalar.copy(rows[0:1, 642:770], pC[0:1, 1, :]), "S")
                sr = p.inc(nc.scalar.copy(rows[0:1, 771:899], pC[0:1, 2, :]), "S")
                p.wait(nc.tensor, "S", sr)
                nc.tensor.matmul(pC[:, 0, :], cst.ones_row_f32[:],
                                 rows[0:1, 514:642], start=True, stop=True)
                nc.tensor.matmul(pC[:, 1, :], cst.ones_row_f32[:],
                                 rows[0:1, 642:770], start=True, stop=True)
                tbx = p.inc(nc.tensor.matmul(pC[:, 2, :], cst.ones_row_f32[:],
                                             rows[0:1, 771:899], start=True,
                                             stop=True), "T")
                p.wait(nc.scalar, "T", tbx)
                p.inc(nc.scalar.copy(rgB[:], pC[:, 0, :]), "S")
                p.inc(nc.scalar.copy(agB[:], pC[:, 1, :]), "S")
                sx = p.inc(nc.scalar.copy(NbB[:], pC[:, 2, :]), "S")
                VW()
                p.inc(nc.vector.tensor_scalar(t1[:], agB[:], cols[:, 2:3], None,
                                              ALU.subtract), "V")
                p.wait(nc.vector, "V")
                p.inc(nc.vector.tensor_mul(t1[:], t1[:], cst.triu_strict[:]), "V")
                SW()
                p.inc(nc.scalar.activation(t1[:], t1[:], AF.Exp), "S")
                VW()
                p.inc(nc.vector.tensor_mul(ETz[:], t1[:], cst.triu_strict[:]),
                      "V")
                p.wait(nc.vector, "V")
                p.inc(nc.vector.tensor_scalar(t2[:], rgB[:], cols[:, 2:3], None,
                                              ALU.subtract), "V")
                p.wait(nc.vector, "V")
                p.inc(nc.vector.tensor_mul(t2[:], t2[:], cst.triu_inc[:]), "V")
                SW()
                p.inc(nc.scalar.activation(t2[:], t2[:], AF.Exp), "S")
                VW()
                p.inc(nc.vector.tensor_mul(E2T[:], t2[:], cst.triu_inc[:]), "V")
                SW()
                p.inc(nc.scalar.mul(Qg[:], projf[:, 0:128], cols[:, 1:2]), "S")
                TW()
                nc.tensor.transpose(pM[:, 0, :], Ktm[:], cst.idn_f32[:])
                nc.tensor.transpose(pM[:, 1, :], Qg[:], cst.idn_f32[:])
                tt2 = p.inc(nc.tensor.transpose(pM[:, 2, :], projf[:, 0:128],
                                                cst.idn_f32[:]), "T")
                p.wait(nc.scalar, "T", tt2)
                p.inc(nc.scalar.copy(Kfm[:], pM[:, 0, :]), "S")
                p.inc(nc.scalar.copy(Qgf[:], pM[:, 1, :]), "S")
                kq = p.inc(nc.scalar.copy(Qfm[:], pM[:, 2, :]), "S")
                p.wait(nc.tensor, "S", kq)
                tkk = p.inc(nc.tensor.matmul(pM[:, 3, :], Kfm[:], Kfm[:],
                                             start=True, stop=True), "T")
                p.wait(nc.vector, "T", tkk)
                p.wait(nc.vector, "S", p.val("S"))
                p.inc(nc.vector.tensor_mul(JT[:], pM[:, 3, :], ETz[:]), "V")
                p.wait(nc.vector, "V")
                p.inc(nc.vector.tensor_mul(JT[:], JT[:], NbB[:]), "V")
                TW()
                tp0 = p.inc(nc.tensor.transpose(pM[:, 3, :], JT[:],
                                                cst.idn_f32[:]), "T")
                p.wait(nc.scalar, "T", tp0)
                p.inc(nc.scalar.copy(Pm[:], pM[:, 3, :]), "S")
                VW()
                p.inc(nc.vector.tensor_copy(PT[:], JT[:]), "V")
                p.wait(nc.vector, "V")
                p.inc(nc.vector.tensor_add(Sm[:], Pm[:], cst.idn_f32[:]), "V")
                p.wait(nc.vector, "V")
                p.inc(nc.vector.tensor_add(STm[:], JT[:], cst.idn_f32[:]), "V")
                for it in range(6):
                    TW()
                    nc.tensor.matmul(pN[:, 0, :], PT[:], Pm[:], start=True,
                                     stop=True)
                    tnn = p.inc(nc.tensor.matmul(pN[:, 1, :], Pm[:], PT[:],
                                                 start=True, stop=True), "T")
                    p.wait(nc.scalar, "T", tnn)
                    p.inc(nc.scalar.copy(Pm[:], pN[:, 0, :]), "S")
                    pc2 = p.inc(nc.scalar.copy(PT[:], pN[:, 1, :]), "S")
                    p.wait(nc.tensor, "S", pc2)
                    nc.tensor.matmul(pN[:, 2, :], PT[:], Sm[:], start=True,
                                     stop=True)
                    tss = p.inc(nc.tensor.matmul(pN[:, 3, :], Sm[:], PT[:],
                                                 start=True, stop=True), "T")
                    p.wait(nc.vector, "T", tss)
                    p.inc(nc.vector.tensor_add(Sm[:], Sm[:], pN[:, 2, :]), "V")
                    vs2 = p.inc(nc.vector.tensor_add(STm[:], STm[:],
                                                     pN[:, 3, :]), "V")
                    p.wait(nc.tensor, "V", vs2)
                TW()
                tkz = p.inc(nc.tensor.matmul(pM[:, 0, :], Kfm[:], Zs[:],
                                             start=True, stop=True), "T")
                p.wait(nc.scalar, "T", tkz)
                p.inc(nc.scalar.mul(t1[:], projf[:, 256:384], be), "S")
                ts1 = p.inc(nc.scalar.mul(t2[:], pM[:, 0, :], cols[:, 10:11]),
                            "S")
                p.wait(nc.vector, "S", ts1)
                vr = p.inc(nc.vector.tensor_add(Rm[:], t1[:], t2[:]), "V")
                p.wait(nc.tensor, "V", vr)
                nc.tensor.matmul(pM[:, 1, :], STm[:], Rm[:], start=True,
                                 stop=True)
                tw2 = p.inc(nc.tensor.matmul(pM[:, 2, :], Rm[:], STm[:],
                                             start=True, stop=True), "T")
                p.wait(nc.scalar, "T", tw2)
                p.inc(nc.scalar.copy(Wtm[:], pM[:, 1, :]), "S")
                sw2 = p.inc(nc.scalar.copy(Wfm[:], pM[:, 2, :]), "S")
                p.wait(nc.scalar, "S", sw2)
                p.inc(nc.scalar.mul(Wg[:], Wtm[:], cols[:, 6:7]), "S")
                TW()
                tt1 = p.inc(nc.tensor.matmul(pM[:, 3, :], Wfm[:], Qfm[:],
                                             start=True, stop=True), "T")
                p.wait(nc.vector, "T", tt1)
                p.wait(nc.vector, "S", p.val("S"))
                vpt = p.inc(nc.vector.tensor_mul(PTm[:], pM[:, 3, :], E2T[:]),
                            "V")
                p.wait(nc.tensor, "V", vpt)
                nc.tensor.matmul(pC[:, 3, :], Qgf[:], ZT[:], start=True,
                                 stop=False)
                to = p.inc(nc.tensor.matmul(pC[:, 3, :], PTm[:], Ktm[:],
                                            start=False, stop=True), "T")
                p.wait(nc.vector, "T", to)
                vo = p.inc(nc.vector.tensor_mul(ogb[:], pC[:, 3, :],
                                                projf[:, 384:512]), "V")
                p.wait(nc.tensor, "V", vo)
                tog = p.inc(nc.tensor.transpose(pBb[:, 0, :], ogb[:],
                                                cst.idn_bf[:]), "T")
                p.wait(nc.scalar, "T", tog)
                sog = p.inc(nc.scalar.copy(ogf[:], pBb[:, 0, :]), "S")
                if dbg is not None and b == 0 and ck == 0:
                    p.wait(nc.sync, "V", p.val("V"))
                    p.wait(nc.sync, "S", p.val("S"))
                    p.wait(nc.sync, "T", p.val("T"))
                    for nm, ap in [("dbg_projf", projf[:]), ("dbg_Ktm", Ktm[:]),
                                   ("dbg_cols", cols[:]), ("dbg_JT", JT[:]),
                                   ("dbg_STm", STm[:]), ("dbg_Rm", Rm[:]),
                                   ("dbg_Wtm", Wtm[:]), ("dbg_Qgf", Qgf[:]),
                                   ("dbg_PTm", PTm[:])]:
                        p.inc(nc.sync.dma_start(dbg[nm], ap), "Sy", 16)
                    p.wait(nc.gpsimd, "V", p.val("V"))
                    p.inc(nc.gpsimd.dma_start(dbg["dbg_ogb"], ogb[:]), "G", 16)
                p.wait(nc.tensor, "S", sog)
                for m in range(NFT):
                    ins = nc.tensor.matmul(pR[:, m, :], wol[:, m, :], ogf[:],
                                           start=(m % 4 == 0), stop=True)
                tfin = p.inc(ins, "T")
                lx = p.inc(nc.sync.dma_start(
                    xr[:], srcv[:, :, tc:tc + 128]), "Sy", 16)
                p.wait(nc.vector, "T", tfin)
                p.wait(nc.vector, "Sy", lx)
                for m in range(NFT):
                    ins = nc.vector.scalar_tensor_tensor(
                        yr[:, m, :], pR[:, m, :], bo[:, m:m + 1], xr[:, m, :],
                        ALU.add, ALU.add)
                vy = p.inc(ins, "V")
                p.wait(nc.sync, "V", vy)
                sy = p.inc(nc.sync.dma_start(dstv[:, :, tc:tc + 128], yr[:]),
                           "Sy", 16)
                p.wait(nc.sync, "Sy", sy)
                TW()
                nc.tensor.matmul(pM[:, 0, :], Ktm[:], Wg[:], start=True,
                                 stop=True)
                tz = p.inc(nc.tensor.matmul(pM[:, 1, :], Wg[:], Ktm[:],
                                            start=True, stop=True), "T")
                p.wait(nc.scalar, "T", tz)
                p.inc(nc.scalar.mul(t1[:], Zs[:], cols[:, 11:12]), "S")
                sz = p.inc(nc.scalar.mul(t2[:], ZT[:], cols[:, 11:12]), "S")
                p.wait(nc.vector, "S", sz)
                p.inc(nc.vector.tensor_add(Zs[:], t1[:], pM[:, 0, :]), "V")
                zu = p.inc(nc.vector.tensor_add(ZT[:], t2[:], pM[:, 1, :]), "V")
                p.wait(nc.tensor, "V", zu)
            p.barrier()


def build_s5(debug=False):
    nc = bass.Bass("TRN2", target_bir_lowering=False)
    ctx = contextlib.ExitStack()
    with ctx:
        p = PB(nc, ctx)
        E = ctx.enter_context
        src = nc.dram_tensor("x_fm", [D, NT], F32, kind="ExternalInput").ap()
        dst = nc.dram_tensor("y_fm", [D, NT], F32, kind="ExternalOutput").ap()
        ws = {k: nc.dram_tensor(k, *WEIGHT_SPECS[k], kind="ExternalInput").ap()
              for k in ["l1a_w4T", "l1a_b4", "l1a_wabT", "l1a_bab", "l1a_woT",
                        "l1a_bo_fm"]}
        cst = Consts(p, E, ["ones_col_bf", "ones_row_f32", "idn_f32", "idn_bf",
                            "triu_inc", "triu_strict", "e127"])
        cst.barrier(p)
        dbg = None
        if debug:
            shp = {"dbg_projf": ([128, 512], F32), "dbg_Ktm": ([128, 128], F32),
                   "dbg_cols": ([128, 12], F32), "dbg_JT": ([128, 128], F32),
                   "dbg_STm": ([128, 128], F32), "dbg_Rm": ([128, 128], F32),
                   "dbg_Wtm": ([128, 128], F32), "dbg_Qgf": ([128, 128], F32),
                   "dbg_PTm": ([128, 128], F32), "dbg_ogb": ([128, 128], F32)}
            dbg = {k: nc.dram_tensor(k, *v, kind="ExternalOutput").ap()
                   for k, v in shp.items()}
        stage_l1attn(p, E, cst, src, dst, ws["l1a_w4T"], ws["l1a_b4"],
                     ws["l1a_wabT"], ws["l1a_bab"], ws["l1a_woT"],
                     ws["l1a_bo_fm"], dbg=dbg)
        p.end()
    return nc




class _CstOverlay:
    def __init__(self, base, local):
        self._b, self._l = base, local

    def __getattr__(self, n):
        if n.startswith("_"):
            raise AttributeError(n)
        if n in self._l:
            return self._l[n]
        return getattr(self._b, n)


def stage_l1ffn(p, E, cst, src, dst, d_xi, w, nlayers=6, aux_dram=None):
    nc = p.nc
    srcv, dstv = fmv(src), fmv(dst)
    xiv = fmv(d_xi)
    NTIL = T // 128

    if aux_dram is not None:
        local = {}
        for n, an in [("rot_csq", "c_rot_csq"), ("rot_ssq", "c_rot_ssq"),
                      ("rot_cik", "c_rot_cik"), ("rot_sik", "c_rot_sik"),
                      ("am0", "c_am0"), ("am1", "c_am1"), ("am2", "c_am2")]:
            shp, dt = AUX_SPEC[an]
            sb = E(nc.sbuf_tensor("lcst_" + n, shp, dt))
            p.inc(nc.gpsimd.dma_start(sb[:], aux_dram[an]), "G", 16)
            local[n] = sb
        cst = _CstOverlay(cst, local)
        for eng in (nc.tensor, nc.vector, nc.scalar, nc.gpsimd, nc.sync):
            p.wait(eng, "G")

    with contextlib.ExitStack() as ph:
        Ep = ph.enter_context
        ln = LNTile(p, Ep, cst, "s4ln")
        xf = Ep(nc.sbuf_tensor("s4_xf", [128, 8, 512], F32))
        xo = Ep(nc.sbuf_tensor("s4_xo", [128, 8, 512], F32))
        st_prev = 0
        for it in range(NT // 512):
            t0 = it * 512
            p.wait(nc.sync, "V", ln.prev_v)
            ld = p.inc(nc.sync.dma_start(xf[:], srcv[:, :, t0:t0 + 512]),
                       "Sy", 16)
            v = ln.emit(xf[:], xo[:], {"Sy": ld}, wout={"Sy": st_prev})
            p.wait(nc.sync, "V", v)
            st_prev = p.inc(nc.sync.dma_start(xiv[:, :, t0:t0 + 512], xo[:]),
                            "Sy", 16)
    p.barrier()

    for li in range(nlayers):
        with contextlib.ExitStack() as ph:
            Ep = ph.enter_context
            xs = Ep(nc.sbuf_tensor(f"s4a_xs{li}", [128, 8, T + 2], BF16))
            wqkv = Ep(nc.sbuf_tensor(f"s4a_wqkv{li}", [128, 8, 3072], BF16))
            wol = Ep(nc.sbuf_tensor(f"s4a_wo{li}", [128, 8, 1024], BF16))
            qsr = Ep(nc.sbuf_tensor(f"s4a_qsr{li}", [1, 4, 64], F32))
            texq_c = Ep(nc.sbuf_tensor(f"s4a_tqc{li}", [128, 16, 64], F32))
            texq_s = Ep(nc.sbuf_tensor(f"s4a_tqs{li}", [128, 16, 64], F32))
            texk_c = Ep(nc.sbuf_tensor(f"s4a_tkc{li}", [128, 16, 64], F32))
            texk_s = Ep(nc.sbuf_tensor(f"s4a_tks{li}", [128, 16, 64], F32))
            sqf = Ep(nc.sbuf_tensor(f"s4a_sqf{li}", [128, 512], F32))
            ssb = Ep(nc.sbuf_tensor(f"s4a_ss{li}", [128, 8, 2], F32))
            qw = Ep(nc.sbuf_tensor(f"s4a_qw{li}", [128, 512], F32))
            qr1 = Ep(nc.sbuf_tensor(f"s4a_qr1{li}", [128, 512], F32))
            qrb = Ep(nc.sbuf_tensor(f"s4a_qrb{li}", [128, 512], BF16))
            QF = Ep(nc.sbuf_tensor(f"s4a_QF{li}", [64, 16, 128], BF16))
            KF = Ep(nc.sbuf_tensor(f"s4a_KF{li}", [64, 16, 4, 128], BF16))
            VR = Ep(nc.sbuf_tensor(f"s4a_VR{li}", [128, 4, 1024], BF16))
            atn = Ep(nc.sbuf_tensor(f"s4a_atn{li}", [128, 384], BF16))
            atT = Ep(nc.sbuf_tensor(f"s4a_atT{li}", [128, 3, 128], BF16))
            aof = Ep(nc.sbuf_tensor(f"s4a_aof{li}", [128, 8, 128], BF16))
            rr = Ep(nc.sbuf_tensor(f"s4a_rr{li}", [128, 2], F32))
            xir = Ep(nc.sbuf_tensor(f"s4a_xir{li}", [128, 8, 128], F32))
            yir = Ep(nc.sbuf_tensor(f"s4a_yir{li}", [128, 8, 128], F32))
            pA = Ep(nc.psum_tensor(f"s4a_pA{li}", [128, 4, 512], F32))
            pwo = Ep(nc.psum_tensor(f"s4a_pwo{li}", [128, 8, 128], F32))
            pT = Ep(nc.psum_tensor(f"s4a_pT{li}", [128, 8, 128], BF16))

            p.inc(nc.gpsimd.dma_start(
                wqkv[:], w["wqkv"][li].rearrange("(c p) n -> p c n", p=128)),
                "G", 16)
            p.inc(nc.gpsimd.dma_start(
                wol[:], w["wo"][li].rearrange("(c p) n -> p c n", p=128)),
                "G", 16)
            p.inc(nc.sync.dma_start(qsr[0:1, 0, :], w["qs"][li]), "Sy", 16)
            ldq = p.inc(nc.sync.dma_start(qsr[0:1, 1, :], w["ks"][li]),
                        "Sy", 16)
            p.wait(nc.scalar, "Sy", ldq)
            p.inc(nc.scalar.copy(qsr[0:1, 2, 0:32], qsr[0:1, 0, 32:64]), "S")
            p.inc(nc.scalar.copy(qsr[0:1, 2, 32:64], qsr[0:1, 0, 0:32]), "S")
            p.inc(nc.scalar.copy(qsr[0:1, 3, 0:32], qsr[0:1, 1, 32:64]), "S")
            qsw = p.inc(nc.scalar.copy(qsr[0:1, 3, 32:64], qsr[0:1, 1, 0:32]),
                        "S")
            p.wait(nc.tensor, "S", qsw)
            p.wait(nc.tensor, "V", p.val("V"))
            nc.tensor.matmul(pA[:, 0, 0:64], cst.row8_f32[:], qsr[0:1, 0, :],
                             start=True, stop=True)
            nc.tensor.matmul(pA[:, 0, 64:128], cst.row8_f32[:], qsr[0:1, 2, :],
                             start=True, stop=False)
            nc.tensor.matmul(pA[:, 1, 0:64], cst.ones_row_f32[:],
                             qsr[0:1, 1, :], start=True, stop=True)
            tq0 = p.inc(nc.tensor.matmul(pA[:, 1, 64:128], cst.ones_row_f32[:],
                                         qsr[0:1, 3, :], start=False,
                                         stop=True), "T")
            p.wait(nc.vector, "T", tq0)
            b3 = lambda ap: ap.rearrange("p (o d) -> p o d", o=1
                                         ).broadcast_to([128, 16, 64])
            p.inc(nc.vector.tensor_mul(texq_c[:], cst.rot_csq[:],
                                       b3(pA[:, 0, 0:64])), "V")
            p.inc(nc.vector.tensor_mul(texq_s[:], cst.rot_ssq[:],
                                       b3(pA[:, 0, 64:128])), "V")
            p.inc(nc.vector.tensor_mul(texk_c[:], cst.rot_cik[:],
                                       b3(pA[:, 1, 0:64])), "V")
            texok = p.inc(nc.vector.tensor_mul(texk_s[:], cst.rot_sik[:],
                                               b3(pA[:, 1, 64:128])), "V")

            for b in range(BPC):
                p.wait(nc.vector, "T", p.val("T"))
                zv = p.inc(nc.vector.memzero(xs[:, :, 0:2]), "V")
                p.wait(nc.gpsimd, "V", zv)
                p.wait(nc.gpsimd, "Sy", p.val("Sy"))
                xsok = p.inc(nc.gpsimd.dma_start(
                    xs[:, :, 2:T + 2], xiv[:, :, b * T:(b + 1) * T]), "G", 16)
                for t in range(NTIL):
                    base = 2 + t * 128
                    slot = t % 4
                    p.wait(nc.tensor, "G", max(xsok, p.val("G")))
                    p.wait(nc.tensor, "V", p.val("V"))
                    for s6 in range(6):
                        pk = s6 % 2
                        p.wait(nc.tensor, "S", p.val("S"))
                        p.wait(nc.tensor, "V", p.val("V"))
                        for c in range(NFT):
                            off = 0 if c < 4 else -1
                            ins = nc.tensor.matmul(
                                pA[:, pk, :],
                                xs[:, c, base + off:base + off + 128],
                                wqkv[:, c, s6 * 512:(s6 + 1) * 512],
                                start=(c == 0), stop=(c == 7))
                        tqk = p.inc(ins, "T")
                        if s6 >= 4:
                            p.wait(nc.scalar, "T", tqk)
                            p.inc(nc.scalar.copy(
                                VR[:, slot, (s6 - 4) * 512:(s6 - 3) * 512],
                                pA[:, pk, :]), "S")
                            continue
                        p.wait(nc.scalar, "T", tqk)
                        p.wait(nc.scalar, "V", p.val("V"))
                        sv = p.inc(nc.scalar.activation(sqf[:], pA[:, pk, :],
                                                        AF.Square), "S")
                        p.wait(nc.vector, "S", sv)
                        p.inc(nc.vector.reduce_sum(
                            ssb[:, :, 0:1], sqf[:].rearrange(
                                "p (h d) -> p h d", d=64), axis=AX.X), "V")
                        p.wait(nc.vector, "V")
                        vsm = p.inc(nc.vector.tensor_scalar_max(
                            ssb[:, :, 0:1], ssb[:, :, 0:1], 1e-24), "V")
                        p.wait(nc.scalar, "V", vsm)
                        p.inc(nc.scalar.activation(ssb[:, :, 1:2],
                                                   ssb[:, :, 0:1], AF.Ln), "S")
                        p.wait(nc.scalar, "S", p.val("S"))
                        srs = p.inc(nc.scalar.activation(
                            ssb[:, :, 0:1], ssb[:, :, 1:2], AF.Exp,
                            scale=-0.5), "S")
                        p.wait(nc.vector, "S", srs)
                        p.inc(nc.vector.tensor_tensor(
                            qw[:].rearrange("p (h d) -> p h d", d=64),
                            pA[:, pk, :].rearrange("p (h d) -> p h d", d=64),
                            ssb[:, :, 0:1].broadcast_to([128, 8, 64]),
                            ALU.mult), "V")
                        tc_ = texq_c if s6 < 2 else texk_c
                        ts_ = texq_s if s6 < 2 else texk_s
                        q3 = qw[:].rearrange("p (h d) -> p h d", d=64)
                        r3 = qr1[:].rearrange("p (h d) -> p h d", d=64)
                        o3 = qrb[:].rearrange("p (h d) -> p h d", d=64)
                        tcs = tc_[:, t, :].rearrange("p (o d) -> p o d", o=1
                                                     ).broadcast_to([128, 8, 64])
                        tss = ts_[:, t, :].rearrange("p (o d) -> p o d", o=1
                                                     ).broadcast_to([128, 8, 64])
                        p.wait(nc.vector, "V")
                        p.inc(nc.vector.tensor_tensor(r3, q3, tcs, ALU.mult),
                              "V")
                        p.wait(nc.vector, "V")
                        p.inc(nc.vector.tensor_tensor(
                            o3[:, :, 0:32], q3[:, :, 32:64], tss[:, :, 0:32],
                            ALU.mult), "V")
                        p.wait(nc.vector, "V")
                        p.inc(nc.vector.tensor_tensor(
                            o3[:, :, 32:64], q3[:, :, 0:32], tss[:, :, 32:64],
                            ALU.mult), "V")
                        p.wait(nc.vector, "V")
                        vrot = p.inc(nc.vector.tensor_add(qrb[:], qrb[:],
                                                          qr1[:]), "V")
                        p.wait(nc.tensor, "V", vrot)
                        p.wait(nc.tensor, "S", p.val("S"))
                        for hh in range(8):
                            ins = nc.tensor.transpose(
                                pT[0:64, hh, :], qrb[:, hh * 64:(hh + 1) * 64],
                                cst.idn_bf[:])
                        ttr = p.inc(ins, "T")
                        p.wait(nc.scalar, "T", ttr)
                        hof = (s6 % 2) * 8
                        if s6 < 2:
                            p.inc(nc.scalar.copy(
                                QF[0:64, hof:hof + 8, :], pT[0:64, :, :]), "S")
                        else:
                            p.inc(nc.scalar.copy(
                                KF[0:64, hof:hof + 8, slot, :],
                                pT[0:64, :, :]), "S")
                    qkv_ok = p.val("S")
                    nch = min(t + 1, 3)
                    mask = [cst.am0, cst.am1, cst.am2][nch - 1]
                    slots = [(t - (nch - 1) + j) % 4 for j in range(nch)]
                    for h in range(16):
                        p.wait(nc.tensor, "S", p.val("S"))
                        p.wait(nc.tensor, "V", p.val("V"))
                        for j, sl in enumerate(slots):
                            nc.tensor.matmul(
                                pA[:, 2, j * 128:(j + 1) * 128],
                                QF[0:64, h, :], KF[0:64, h, sl, :],
                                start=(j == 0), stop=False)
                        tsc = p.inc(nc.tensor.matmul(
                            pA[:, 2, 0:nch * 128], cst.idn_bf[:],
                            mask[:], start=False, stop=True), "T")
                        p.wait(nc.vector, "T", tsc)
                        p.wait(nc.vector, "S", p.val("S"))
                        nm = p.inc(nc.vector.reduce_max(
                            rr[:, 0:1], pA[:, 2, 0:nch * 128], axis=AX.X,
                            negate=True), "V")
                        p.wait(nc.scalar, "V", nm)
                        ex = p.inc(nc.scalar.activation(
                            atn[:, 0:nch * 128], pA[:, 2, 0:nch * 128], AF.Exp,
                            bias=rr[:, 0:1], accum_out=rr[:, 1:2]), "S")
                        p.wait(nc.vector, "S", ex)
                        rv = p.inc(nc.vector.reciprocal(rr[:, 1:2], rr[:, 1:2]),
                                   "V")
                        p.wait(nc.scalar, "V", rv)
                        an = p.inc(nc.scalar.mul(atn[:, 0:nch * 128],
                                                 atn[:, 0:nch * 128],
                                                 rr[:, 1:2]), "S")
                        p.wait(nc.tensor, "S", an)
                        for j in range(nch):
                            ins = nc.tensor.transpose(
                                pT[:, j, :], atn[:, j * 128:(j + 1) * 128],
                                cst.idn_bf[:])
                        ttt = p.inc(ins, "T")
                        p.wait(nc.scalar, "T", ttt)
                        at2 = p.inc(nc.scalar.copy(atT[:, 0:nch, :],
                                                   pT[:, 0:nch, :]), "S")
                        p.wait(nc.tensor, "S", at2)
                        for j, sl in enumerate(slots):
                            ins = nc.tensor.matmul(
                                pA[0:64, 3, 0:128],
                                VR[:, sl, h * 64:(h + 1) * 64], atT[:, j, :],
                                start=(j == 0), stop=(j == nch - 1))
                        tout = p.inc(ins, "T")
                        p.wait(nc.scalar, "T", tout)
                        p.inc(nc.scalar.copy(
                            aof[(h % 2) * 64:(h % 2) * 64 + 64, h // 2, :],
                            pA[0:64, 3, 0:128]), "S")
                    aok = p.val("S")
                    p.wait(nc.tensor, "S", aok)
                    p.wait(nc.tensor, "V", p.val("V"))
                    for m in range(NFT):
                        for kc in range(NFT):
                            ins = nc.tensor.matmul(
                                pwo[:, m, :], wol[:, kc, m * 128:(m + 1) * 128],
                                aof[:, kc, :], start=(kc == 0 and m % 4 == 0),
                                stop=(kc == 7))
                    two = p.inc(ins, "T")
                    tg = b * T + t * 128
                    lx = p.inc(nc.sync.dma_start(xir[:],
                                                 xiv[:, :, tg:tg + 128]),
                               "Sy", 16)
                    p.wait(nc.vector, "T", two)
                    p.wait(nc.vector, "Sy", lx)
                    p.wait(nc.vector, "Sy")
                    vy = p.inc(nc.vector.tensor_add(
                        yir[:].rearrange("p c n -> p (c n)"),
                        pwo[:].rearrange("p c n -> p (c n)"),
                        xir[:].rearrange("p c n -> p (c n)")), "V")
                    p.wait(nc.sync, "V", vy)
                    p.inc(nc.sync.dma_start(xiv[:, :, tg:tg + 128], yir[:]),
                          "Sy", 16)
        p.barrier()

        with contextlib.ExitStack() as ph:
            Ep = ph.enter_context
            w1s = Ep(nc.sbuf_tensor(f"s4f_w1{li}", [128, 8, 5460], BF16))
            w2s = Ep(nc.sbuf_tensor(f"s4f_w2{li}", [128, 22, 1024], BF16))
            lgb = Ep(nc.sbuf_tensor(f"s4f_lgb{li}", [128, 2, 8], F32))
            b1s = Ep(nc.sbuf_tensor(f"s4f_b1{li}", [128, 2, 22], F32))
            b2s = Ep(nc.sbuf_tensor(f"s4f_b2{li}", [128, 8], F32))
            ln2 = LNTile(p, Ep, cst, f"s4f_ln{li}", maxw=256)
            xff = Ep(nc.sbuf_tensor(f"s4f_xff{li}", [128, 8, 256], F32))
            xn = Ep(nc.sbuf_tensor(f"s4f_xn{li}", [128, 8, 256], BF16))
            a1 = Ep(nc.sbuf_tensor(f"s4f_a1{li}", [128, 256], F32))
            g1 = Ep(nc.sbuf_tensor(f"s4f_g1{li}", [128, 256], F32))
            act = Ep(nc.sbuf_tensor(f"s4f_act{li}", [128, 256], BF16))
            xir = Ep(nc.sbuf_tensor(f"s4f_xir{li}", [128, 8, 256], F32))
            yir = Ep(nc.sbuf_tensor(f"s4f_yir{li}", [128, 8, 256], F32))
            svc = Ep(nc.sbuf_tensor(f"s4f_svc{li}", [128, 4, 1], F32))
            pH = Ep(nc.psum_tensor(f"s4f_pH{li}", [128, 2, 256], F32))
            pO = Ep(nc.psum_tensor(f"s4f_pO{li}", [128, 8, 256], F32))

            p.inc(nc.gpsimd.dma_start(
                w1s[:], w["w1"][li].rearrange("(c p) n -> p c n", p=128)),
                "G", 16)
            p.inc(nc.gpsimd.dma_start(
                w2s[0:128, 0:21, :],
                w["w2"][li, 0:2688].rearrange("(c p) n -> p c n", p=128)),
                "G", 16)
            p.inc(nc.gpsimd.dma_start(w2s[0:42, 21, :], w["w2"][li, 2688:2730]),
                  "G", 16)
            p.inc(nc.sync.dma_start(lgb[:, 0, :], w["lng"][li]), "Sy", 16)
            p.inc(nc.sync.dma_start(lgb[:, 1, :], w["lnb"][li]), "Sy", 16)
            p.inc(nc.sync.dma_start(b1s[:, 0, :], w["b1a"][li]), "Sy", 16)
            p.inc(nc.sync.dma_start(b1s[:, 1, :], w["b1g"][li]), "Sy", 16)
            ldw = p.inc(nc.sync.dma_start(b2s[:], w["b2"][li]), "Sy", 16)
            p.wait(nc.scalar, "Sy", ldw)
            p.wait(nc.vector, "Sy", ldw)
            p.wait(nc.tensor, "G", p.val("G"))

            for b in range(BPC):
                for tt in range(T // 256):
                    tg = b * T + tt * 256
                    p.wait(nc.sync, "V", ln2.prev_v)
                    if tt == 0:
                        zm = p.inc(nc.vector.memzero(xff[:, 4:8, 0:1]), "V")
                    else:
                        p.wait(nc.vector, "S", p.val("S"))
                        zm = p.inc(nc.vector.tensor_copy(xff[:, 4:8, 0:1],
                                                         svc[:]), "V")
                    p.wait(nc.sync, "V", zm)
                    p.inc(nc.sync.dma_start(
                        xff[:, 0:4, :], xiv[:, 0:4, tg:tg + 256]), "Sy", 16)
                    ldx = p.inc(nc.sync.dma_start(
                        xff[:, 4:8, 1:256], xiv[:, 4:8, tg:tg + 255]),
                        "Sy", 16)
                    sxn = ln2.emit(xff[:], xn[:], {"Sy": ldx}, width=256,
                                   gb=(lgb[:, 0, :], lgb[:, 1, :]),
                                   wout={"T": p.val("T")})
                    p.wait(nc.tensor, "V", sxn)
                    first = [True] * 4
                    for hc in range(22):
                        hw = 128 if hc < 21 else 42
                        ca = hc * 128
                        cg = 2730 + hc * 128
                        p.wait(nc.tensor, "S", p.val("S"))
                        for c in range(NFT):
                            nc.tensor.matmul(
                                pH[0:hw, 0, :], w1s[:, c, ca:ca + hw],
                                xn[:, c, :], start=(c == 0), stop=(c == 7))
                        for c in range(NFT):
                            ins = nc.tensor.matmul(
                                pH[0:hw, 1, :], w1s[:, c, cg:cg + hw],
                                xn[:, c, :], start=(c == 0), stop=(c == 7))
                        th = p.inc(ins, "T")
                        p.wait(nc.scalar, "T", th)
                        p.wait(nc.scalar, "V", p.val("V"))
                        p.inc(nc.scalar.activation(
                            g1[0:hw, :], pH[0:hw, 1, :], AF.Gelu,
                            bias=b1s[0:hw, 1, hc:hc + 1]), "S")
                        p.wait(nc.vector, "T", th)
                        p.wait(nc.vector, "S", p.val("S"))
                        sa = p.inc(nc.vector.tensor_scalar_add(
                            a1[0:hw, :], pH[0:hw, 0, :],
                            b1s[0:hw, 0, hc:hc + 1]), "V")
                        p.wait(nc.vector, "V", sa)
                        va = p.inc(nc.vector.tensor_mul(act[0:hw, :],
                                                        a1[0:hw, :],
                                                        g1[0:hw, :]), "V")
                        p.wait(nc.tensor, "V", va)
                        for m in range(NFT):
                            bank = m // 2
                            ins = nc.tensor.matmul(
                                pO[:, m, :], w2s[0:hw, hc, m * 128:(m + 1) * 128],
                                act[0:hw, :], start=first[bank], stop=False)
                            first[bank] = False
                        p.inc(ins, "T")
                    lx = p.inc(nc.sync.dma_start(xir[:],
                                                 xiv[:, :, tg:tg + 256]),
                               "Sy", 16)
                    p.wait(nc.vector, "T", p.val("T"))
                    p.wait(nc.vector, "Sy", lx)
                    p.wait(nc.vector, "Sy")
                    p.inc(nc.vector.tensor_copy(svc[:],
                                                xir[:, 4:8, 255:256]), "V")
                    p.wait(nc.vector, "V")
                    for m in range(NFT):
                        ins = nc.vector.scalar_tensor_tensor(
                            yir[:, m, :], pO[:, m, :], b2s[:, m:m + 1],
                            xir[:, m, :], ALU.add, ALU.add)
                    vy = p.inc(ins, "V")
                    first = [True] * 4
                    p.wait(nc.sync, "V", vy)
                    p.inc(nc.sync.dma_start(xiv[:, :, tg:tg + 256], yir[:]),
                          "Sy", 16)
        p.barrier()

    with contextlib.ExitStack() as ph:
        Ep = ph.enter_context
        xa = Ep(nc.sbuf_tensor("s4z_xa", [128, 8, 512], F32))
        aa = Ep(nc.sbuf_tensor("s4z_aa", [128, 8, 512], F32))
        yy = Ep(nc.sbuf_tensor("s4z_yy", [128, 8, 512], F32))
        vprev = 0
        sprev = 0
        for it in range(NT // 512):
            t0 = it * 512
            p.wait(nc.sync, "V", vprev)
            lx = p.inc(nc.sync.dma_start(xa[:], srcv[:, :, t0:t0 + 512]),
                       "Sy", 16)
            la = p.inc(nc.sync.dma_start(aa[:], xiv[:, :, t0:t0 + 512]),
                       "Sy", 16)
            p.wait(nc.vector, "Sy", la)
            p.wait(nc.vector, "Sy", sprev)
            for c in range(NFT):
                ins = nc.vector.scalar_tensor_tensor(
                    yy[:, c, :], aa[:, c, :], 0.5, xa[:, c, :],
                    ALU.mult, ALU.add)
            vprev = p.inc(ins, "V")
            p.wait(nc.sync, "V", vprev)
            sprev = p.inc(nc.sync.dma_start(dstv[:, :, t0:t0 + 512], yy[:]),
                          "Sy", 16)


def build_s4(nlayers=6):
    nc = bass.Bass("TRN2", target_bir_lowering=False)
    ctx = contextlib.ExitStack()
    with ctx:
        p = PB(nc, ctx)
        E = ctx.enter_context
        src = nc.dram_tensor("x_fm", [D, NT], F32, kind="ExternalInput").ap()
        dst = nc.dram_tensor("y_fm", [D, NT], F32, kind="ExternalOutput").ap()
        d_xi = nc.dram_tensor("d_xi", [D, NT], F32, kind="Internal").ap()
        names = {"wqkv": "l1f_wqkv", "wo": "l1f_wo", "w1": "l1f_w1",
                 "w2": "l1f_w2", "qs": "l1f_qs", "ks": "l1f_ks",
                 "lng": "l1f_lng_fm", "lnb": "l1f_lnb_fm", "b1a": "l1f_b1a",
                 "b1g": "l1f_b1g", "b2": "l1f_b2_fm"}
        w = {k: nc.dram_tensor(v, *WEIGHT_SPECS[v], kind="ExternalInput").ap()
             for k, v in names.items()}
        cst = Consts(p, E, ["ones_col_bf", "ones_row_f32", "row8_f32",
                            "idn_bf", "rot_csq", "rot_ssq", "rot_cik",
                            "rot_sik", "am0", "am1", "am2"])
        cst.barrier(p)
        stage_l1ffn(p, E, cst, src, dst, d_xi, w, nlayers=nlayers)
        p.end()
    return nc




class _NCProxy:

    def __init__(self, nc, sfx):
        object.__setattr__(self, "_nc", nc)
        object.__setattr__(self, "_sfx", sfx)

    def __getattr__(self, n):
        return getattr(self._nc, n)

    def sbuf_tensor(self, name, shape, dtype):
        return self._nc.sbuf_tensor(name + self._sfx, shape, dtype)

    def psum_tensor(self, name, shape, dtype):
        return self._nc.psum_tensor(name + self._sfx, shape, dtype)


def build_fused():
    nc = bass.Bass("TRN2", target_bir_lowering=False)
    ctx = contextlib.ExitStack()
    with ctx:
        p = PB(nc, ctx)
        E = ctx.enter_context
        src = nc.dram_tensor("x_fm", [D, NT], F32, kind="ExternalInput").ap()
        dst = nc.dram_tensor("y_fm", [D, NT], F32, kind="ExternalOutput").ap()
        ws = {k: nc.dram_tensor(k, *WEIGHT_SPECS[k], kind="ExternalInput").ap()
              for k in WEIGHT_SPECS}
        d_xln = nc.dram_tensor("d_xln", [D, NT], BF16, kind="Internal").ap()
        d_acc = nc.dram_tensor("d_acc", [D, NT], F32, kind="Internal").ap()
        d_grows = nc.dram_tensor("d_grows", [16, NT], F32, kind="Internal").ap()
        d_xi = nc.dram_tensor("d_xi", [D, NT], F32, kind="Internal").ap()
        cst = Consts(p, E, ["ones_col_f32", "ones_col_bf", "ones_row_f32",
                            "ones_row_bf", "row8_f32", "idn_f32", "idn_bf",
                            "triu_inc", "triu_strict", "e127"])
        aux_dram = {an: nc.dram_tensor(an, *AUX_SPEC[an],
                                       kind="ExternalInput").ap()
                    for an in ["c_rot_csq", "c_rot_ssq", "c_rot_cik",
                               "c_rot_sik", "c_am0", "c_am1", "c_am2"]}
        cst.barrier(p)
        w4 = {"wqkv": ws["l1f_wqkv"], "wo": ws["l1f_wo"], "w1": ws["l1f_w1"],
              "w2": ws["l1f_w2"], "qs": ws["l1f_qs"], "ks": ws["l1f_ks"],
              "lng": ws["l1f_lng_fm"], "lnb": ws["l1f_lnb_fm"],
              "b1a": ws["l1f_b1a"], "b1g": ws["l1f_b1g"],
              "b2": ws["l1f_b2_fm"]}
        real = nc
        for i, st in enumerate(["f0", "a0", "f1", "F0", "A1", "F1"]):
            p.nc = _NCProxy(real, "_" + st)
            with contextlib.ExitStack() as sctx:
                Es = sctx.enter_context
                if st in ("f0", "f1"):
                    stage_l0ffn(p, Es, cst, src if st == "f0" else dst, dst,
                                d_xln, d_acc, d_grows, ws["l0f_up"],
                                ws["l0f_down"], ws["l0f_rw"])
                elif st == "a0":
                    stage_l0attn(p, Es, cst, dst, dst, ws["l0a_wqT"],
                                 ws["l0a_wkvdT"], ws["l0a_wkuT"],
                                 ws["l0a_wvuT"], ws["l0a_woT"])
                elif st in ("F0", "F1"):
                    stage_l1ffn(p, Es, cst, dst, dst, d_xi, w4,
                                aux_dram=aux_dram)
                else:
                    stage_l1attn(p, Es, cst, dst, dst, ws["l1a_w4T"],
                                 ws["l1a_b4"], ws["l1a_wabT"], ws["l1a_bab"],
                                 ws["l1a_woT"], ws["l1a_bo_fm"])
            p.nc = real
            p.barrier()
        p.end()
    return nc


_AUX_S1 = ["c_ones_col_f32", "c_ones_col_bf", "c_ones_row_f32",
           "c_ones_row_bf", "c_idn_f32"]
_AUX_S2 = ["c_ones_col_bf", "c_ones_row_f32", "c_idn_bf"]
_AUX_S4 = ["c_ones_col_bf", "c_ones_row_f32", "c_row8_f32", "c_idn_bf",
           "c_rot_csq", "c_rot_ssq", "c_rot_cik", "c_rot_sik",
           "c_am0", "c_am1", "c_am2"]
_AUX_S5 = ["c_ones_col_bf", "c_ones_row_f32", "c_idn_f32", "c_idn_bf",
           "c_triu_inc_f32", "c_triu_strict_f32", "c_e127_f32"]
_W_S1 = ["l0f_up", "l0f_down", "l0f_rw"]
_W_S2 = ["l0a_wqT", "l0a_wkvdT", "l0a_wkuT", "l0a_wvuT", "l0a_woT"]
_W_S4 = ["l1f_wqkv", "l1f_wo", "l1f_w1", "l1f_w2", "l1f_qs", "l1f_ks",
         "l1f_lng_fm", "l1f_lnb_fm", "l1f_b1a", "l1f_b1g", "l1f_b2_fm"]
_W_S5 = ["l1a_w4T", "l1a_b4", "l1a_wabT", "l1a_bab", "l1a_woT", "l1a_bo_fm"]


def kernel(**inputs):
    import os
    shared, xs = _host_prep(inputs)
    if os.environ.get("MK_UNFUSED", "") != "1":
        nc = build_fused()
        keys = list(WEIGHT_SPECS) + list(AUX_SPEC)
        in_maps = [dict({k: shared[k] for k in keys}, x_fm=xs[c])
                   for c in range(NCORES)]
        res = run_bass_kernel_spmd(nc, in_maps, core_ids=list(range(NCORES)))
        out = np.empty((16, T, D), np.float32)
        for c in range(NCORES):
            out[2 * c:2 * c + 2] = np.ascontiguousarray(
                res.results[c]["y_fm"]).T.reshape(2, T, D)
        return out
    progs = [(build_s1(), _W_S1 + _AUX_S1),
             (build_s2(), _W_S2 + _AUX_S2),
             (build_s4(), _W_S4 + _AUX_S4),
             (build_s5(), _W_S5 + _AUX_S5)]
    order = [0, 1, 0, 2, 3, 2]
    cur = xs
    for pi in order:
        nc, keys = progs[pi]
        in_maps = [dict({k: shared[k] for k in keys}, x_fm=cur[c])
                   for c in range(NCORES)]
        res = run_bass_kernel_spmd(nc, in_maps, core_ids=list(range(NCORES)))
        cur = [np.ascontiguousarray(res.results[c]["y_fm"])
               for c in range(NCORES)]
    out = np.empty((16, T, D), np.float32)
    for c in range(NCORES):
        out[2 * c:2 * c + 2] = cur[c].T.reshape(2, T, D)
    return out
